# version 01
# baseline (speedup 1.0000x reference)
import numpy as np
import ml_dtypes
from contextlib import ExitStack

import concourse.bass as bass
import concourse.mybir as mybir
from concourse.bass_utils import run_bass_kernel_spmd

F32 = mybir.dt.float32
BF16 = mybir.dt.bfloat16
I32 = mybir.dt.int32
AF = mybir.ActivationFunctionType
OP = mybir.AluOpType
AX = mybir.AxisListType.X
BF16NP = ml_dtypes.bfloat16

B = 2
S = 8192
D = 1024
NTOK = 2048
NT = 16
ND = 8
NIB = 16
NJT = 64
NSTG = 4
FFN = 2048
NEG = -1e30
GROUPS = [[0, 1, 2, 3], [4, 5, 6, 7]]


def _perm(x):
    return np.ascontiguousarray(
        x.reshape(B, 16, 512, D).transpose(0, 2, 1, 3).reshape(B, S, D))


def _unperm(x):
    return np.ascontiguousarray(
        x.reshape(B, 512, 16, D).transpose(0, 2, 1, 3).reshape(B, S, D))


def _c(a):
    return np.ascontiguousarray(np.asarray(a), dtype=np.float32)


def _masks_host():
    m = np.arange(4)[:, None, None] * 8 + (np.arange(128) // 16)[None, :, None]
    ii = (np.arange(512) // 16)[None, None, :]
    return np.where(m <= ii, 0.0, NEG).astype(ml_dtypes.bfloat16)


def host_prep(inputs):
    xp = _perm(_c(inputs["L0_attn_x"]))
    wq0 = _c(inputs["L0_attn_Wq"]).T
    wk0 = _c(inputs["L0_attn_Wk"]).T
    wv0 = _c(inputs["L0_attn_Wv"]).T
    wo0a = np.concatenate([_c(inputs["L0_attn_Wo"]).T,
                           _c(inputs["L0_attn_bo"])[None, :]], axis=0)
    bq0 = _c(inputs["L0_attn_bq"])
    bk0 = _c(inputs["L0_attn_bk"])
    bv0 = _c(inputs["L0_attn_bv"])
    gate_aug = np.concatenate([_c(inputs["L0_ffn_gate_w"]),
                               _c(inputs["L0_ffn_gate_b"])[None, :]], axis=0)
    w1_all = np.concatenate([_c(inputs["L0_ffn_rw1"]), _c(inputs["L0_ffn_sw1"])], 0)
    b1_all = np.concatenate([_c(inputs["L0_ffn_rb1"]), _c(inputs["L0_ffn_sb1"])], 0)
    w2_all = np.concatenate([_c(inputs["L0_ffn_rw2"]), _c(inputs["L0_ffn_sw2"])], 0)
    b2_all = np.concatenate([_c(inputs["L0_ffn_rb2"]), _c(inputs["L0_ffn_sb2"])], 0)
    wq1 = _c(inputs["L1_attn_Wq"]).T
    wk1 = _c(inputs["L1_attn_Wk"]).T
    wv1 = _c(inputs["L1_attn_Wv"]).T
    ident = np.eye(128, dtype=BF16NP)
    onesb = np.ones((1, 128), dtype=BF16NP)
    onesc = np.ones((128, 1), dtype=BF16NP)
    onesf = np.ones((128, 64), dtype=np.float32)
    hsel = np.zeros((128, 8), dtype=BF16NP)
    for cc in range(2):
        for hl in range(2):
            hsel[64 * hl:64 * hl + 64, 4 * cc + (2 * cc + hl)] = 1.0
    masks = _masks_host()

    per_core = []
    for c in range(8):
        b, me = c // 4, c % 4
        sl = slice(256 * me, 256 * me + 256)
        per_core.append({
            "x": np.ascontiguousarray(xp[b, 2048 * me:2048 * (me + 1)]),
            "me": np.array([[me]], dtype=np.int32),
            "ident": ident, "onesb": onesb, "onesc": onesc, "onesf": onesf,
            "hsel": hsel, "masks": masks,
            "wq0": np.ascontiguousarray(wq0[:, sl]),
            "wk0": np.ascontiguousarray(wk0[:, sl]),
            "wv0": np.ascontiguousarray(wv0[:, sl]),
            "bq0": np.ascontiguousarray(bq0[sl].reshape(2, 128).T),
            "bk0": np.ascontiguousarray(bk0[sl].reshape(2, 128).T),
            "bv0": np.ascontiguousarray(bv0[None, sl]),
            "wo0": wo0a, "gatew": gate_aug,
            "w1": w1_all,
            "b1": np.ascontiguousarray(
                b1_all.reshape(9, 16, 128).transpose(0, 2, 1)),
            "w2": w2_all, "b2": b2_all,
            "wq1": np.ascontiguousarray(wq1[:, sl]),
            "wk1": np.ascontiguousarray(wk1[:, sl]),
            "wv1": np.ascontiguousarray(wv1[:, sl]),
            "ffw": _c(inputs["L1_ffn_weight"]),
            "ffa1": _c(inputs["L1_ffn_aux_weight1"]),
            "ffa2": _c(inputs["L1_ffn_aux_weight2"]),
            "ffb": np.ascontiguousarray(
                _c(inputs["L1_ffn_bias"]).reshape(8, 128).T),
        })
    return per_core


def _l0_tiles(ib):
    out = []
    for jt in range(NJT):
        d = 8 * jt - 32 * ib
        if d <= -8:
            out.append((jt, None))
        elif d <= 24:
            out.append((jt, d // 8))
    return out


INPUT_SPECS = [
    ("x", [NTOK, D], F32), ("me", [1, 1], I32),
    ("ident", [128, 128], BF16), ("onesb", [1, 128], BF16),
    ("onesc", [128, 1], BF16), ("onesf", [128, 64], F32),
    ("hsel", [128, 8], BF16), ("masks", [4, 128, 512], BF16),
    ("wq0", [D, 256], F32), ("wk0", [D, 256], F32), ("wv0", [D, 256], F32),
    ("bq0", [128, 2], F32), ("bk0", [128, 2], F32), ("bv0", [1, 256], F32),
    ("wo0", [D + 1, D], F32), ("gatew", [D + 1, 8], F32),
    ("w1", [9, D, FFN], F32), ("b1", [9, 128, 16], F32),
    ("w2", [9, D, D], F32), ("b2", [9, D], F32),
    ("wq1", [D, 256], F32), ("wk1", [D, 256], F32), ("wv1", [D, 256], F32),
    ("ffw", [D, D], F32), ("ffa1", [D, D], F32), ("ffa2", [D, D], F32),
    ("ffb", [128, 8], F32),
]


class Prog:
    def __init__(self, layers, limit=None, stage_sub=4, stage_count=NSTG,
                 qk_parts=15):
        self.limit = limit if limit is not None else 99
        self.stage_sub = stage_sub
        self.stage_count = stage_count
        self.qk_parts = qk_parts
        self._phn = 0
        self.layers = list(layers)
        nc = bass.Bass("TRN2", target_bir_lowering=False)
        self.nc = nc
        self.cnt = {}
        self._ctx = ExitStack()
        self._dram()
        self._sbuf()
        self._psum()
        self._sems()
        self.block = self._ctx.enter_context(nc.Block())
        self._emit()
        self._ctx.close()

    def _dram(self):
        nc = self.nc
        self.din = {}
        for n, sh, dt in INPUT_SPECS:
            self.din[n] = nc.dram_tensor(n, sh, dt, kind="ExternalInput").ap()
        self.d_y = nc.dram_tensor("y", [NTOK, D], F32, kind="ExternalOutput").ap()
        self.dsc = {}
        for l in self.layers:
            self.dsc[("xnp", l)] = nc.dram_tensor(f"xnp{l}", [NTOK, D], BF16).ap()
            self.dsc[("xng", l)] = nc.dram_tensor(f"xng{l}", [S, D], BF16).ap()
            self.dsc[("atp", l)] = nc.dram_tensor(f"atp{l}", [256, S], BF16).ap()
            self.dsc[("qres", l)] = nc.dram_tensor(f"qres{l}", [256, S], BF16).ap()
            self.dsc[("atg", l)] = nc.dram_tensor(f"atg{l}", [D, S], BF16).ap()
            self.dsc[("nb", l)] = nc.dram_tensor(f"nb{l}", [8, 1], F32).ap()

    def _sbuf(self):
        E = self._ctx.enter_context
        sb = lambda n, sh, dt: E(self.nc.sbuf_tensor(n, sh, dt))
        self.x = sb("xres", [128, NT * D], F32)
        self.pt = sb("pt", [128, ND * 2048], BF16)
        self.pb = sb("pb", [128, 16384], BF16)
        self.pc = sb("pc", [128, 8192], BF16)
        self.pe_ = sb("pe_", [128, 8192], BF16)
        self.pd = sb("pd", [128, 16384], BF16)
        self.e0 = sb("e0", [128, 1024], BF16)
        self.e1 = sb("e1", [128, 1024], BF16)
        self.qbuf = sb("qbuf", [128, 1024], BF16)
        self.ident = sb("identS", [128, 128], BF16)
        self.onesb = sb("onesbS", [1, 128], BF16)
        self.onesc = sb("onescS", [128, 1], BF16)
        self.onesf = sb("onesfS", [128, 64], F32)
        self.hsel = sb("hselS", [128, 8], BF16)
        self.mes = sb("mes", [1, 1], I32)
        self.bqs = sb("bqs", [128, 2], F32)
        self.bks = sb("bks", [128, 2], F32)
        self.bvrow = sb("bvrow", [1, 256], BF16)
        self.b1s = sb("b1s", [128, 32], F32)
        self.ffbs = sb("ffbs", [128, 8], F32)
        self.rbuf = sb("rbuf", [128, 512], F32)
        self.rrep = sb("rrep", [128, 512], F32)
        self.qn = sb("qn", [4, 2], F32)
        self.kn = sb("kn", [4, 2], F32)
        self.nb2 = sb("nb2", [2, 8], F32)
        self.cb = sb("cb", [128, 2], F32)
        self.ncp = sb("ncp", [2, 2], BF16)
        self.qsq = sb("qsq", [128, 4 * 512], BF16)
        self.stats = self.rbuf[:, 384:480]
        self.b2s = self.pd[0:9, 8192:9216]
        self.gw = self.pd[:, 9216:9280]
        self.grow = self.pd[0:1, 9280:9288]
        self.wqkv = self.pb[:, 0:6144]
        self.msk = self.qsq[:, 0:2048]
        self.cwT = self.qsq[0:9, 0:2048]
        self.gm = self.qsq[:].bitcast(F32)[:, 0:128]
        self.gst = self.stats[:, 0:64]
        self.gl = self.rbuf[:, 0:128]
        self.ge = self.rbuf[:, 128:256]
        self.cw = self.rbuf[:, 256:384]
        self.cw9 = self.rrep[:].bitcast(BF16)[:, 0:256]
        self.worow = self.e0[0:1, 0:1024]
        self.roll = [self.e0[:], self.e1[:]]

    def _psum(self):
        E = self._ctx.enter_context
        ps = lambda n, sh: E(self.nc.psum_tensor(n, sh, F32))
        self.ps0 = ps("ps0", [128, 1024])
        self.ps1 = ps("ps1", [128, 1024])
        self.pav = ps("pav", [128, 512])
        self.psm = ps("psm", [128, 512])
        self.pr = ps("pr", [128, 512])
        self.pg = ps("pg", [128, 512])
        self.rot = [self.ps0[:, 0:512], self.ps0[:, 512:1024],
                    self.ps1[:, 0:512], self.ps1[:, 512:1024]]
        self.trot = [self.pr[:].bitcast(BF16)[:, 0:128],
                     self.pg[:].bitcast(BF16)[:, 0:128],
                     self.pav[:].bitcast(BF16)[:, 0:128],
                     self.psm[:].bitcast(BF16)[:, 0:128]]

    def _sems(self):
        E = self._ctx.enter_context
        self.sem = {}
        for n in ["syn", "gpd", "pe", "act", "dv", "pet"]:
            self.sem[n] = E(self.nc.semaphore(n))
            self.cnt[n] = 0
        self.ccs = {}
        for l in self.layers:
            for tag in ["xn", "at"]:
                self.ccs[(l, tag)] = E(self.nc.semaphore(f"cc_{tag}{l}"))

    def inc(self, inst, s, n=1):
        inst.then_inc(self.sem[s], n)
        self.cnt[s] += n
        return self.cnt[s]

    def w(self, eng, s, v=None):
        eng.wait_ge(self.sem[s], self.cnt[s] if v is None else v)

    def _mm_groups(self, n, emit_mm, evac, pre_wait=None, evac_eng_of=None,
                   rot=None, evac_pre_wait=None, extra_release=None):
        p = self
        rot = rot or p.rot
        eng_of = evac_eng_of or (lambda k: "dv")
        pe0 = p.cnt["pe"]
        pe_val = [pe0 + k + 1 for k in range(n)]
        ev_val = [None] * n
        run = {"dv": p.cnt["dv"], "act": p.cnt["act"]}
        order = {"dv": [], "act": []}
        for k in range(n):
            e = eng_of(k)
            order[e].append(k)
        for e in ["dv", "act"]:
            for j, k in enumerate(order[e]):
                ev_val[k] = (e, run[e] + j + 1)

        @p.block.tensor
        def _(t):
            if pre_wait is not None:
                pre_wait(t)
            else:
                p.w(t, "dv")
                p.w(t, "act")
            for k in range(n):
                if k >= 4:
                    e, v = ev_val[k - 4]
                    p.w(t, e, v)
                    if extra_release is not None:
                        e2, v2 = extra_release(k - 4)
                        p.w(t, e2, v2)
                inst = emit_mm(t, k, rot[k % 4])
                inst.then_inc(p.sem["pe"], 1)
        p.cnt["pe"] = pe_val[-1]

        if order["dv"]:
            @p.block.vector
            def _(v):
                if evac_pre_wait is not None:
                    evac_pre_wait(v)
                for k in order["dv"]:
                    p.w(v, "pe", pe_val[k])
                    evac(v, k, rot[k % 4]).then_inc(p.sem["dv"], 1)
            p.cnt["dv"] += len(order["dv"])
        if order["act"]:
            @p.block.scalar
            def _(a):
                if evac_pre_wait is not None:
                    evac_pre_wait(a)
                for k in order["act"]:
                    p.w(a, "pe", pe_val[k])
                    evac(a, k, rot[k % 4]).then_inc(p.sem["act"], 1)
            p.cnt["act"] += len(order["act"])
        return pe_val, ev_val

    def _transposes(self, pairs, pre_wait=None):
        p = self
        n = len(pairs)
        pe0 = p.cnt["pe"]
        pe_val = [pe0 + i + 1 for i in range(n)]
        dv0, act0 = p.cnt["dv"], p.cnt["act"]
        ev_val = [("dv", dv0 + i // 2 + 1) if i % 2 == 0 else
                  ("act", act0 + i // 2 + 1) for i in range(n)]

        @p.block.tensor
        def _(t):
            if pre_wait is not None:
                pre_wait(t)
            else:
                p.w(t, "dv")
                p.w(t, "act")
            for i, (src, dst) in enumerate(pairs):
                if i >= 4:
                    e, v = ev_val[i - 4]
                    p.w(t, e, v)
                sl = p.trot[i % 4]
                rows, cols = src.shape[0], src.shape[1]
                inst = t.transpose(sl[0:cols, 0:rows], src,
                                   p.ident[0:rows, 0:rows])
                inst.then_inc(p.sem["pe"], 1)
        p.cnt["pe"] = pe_val[-1]

        @p.block.vector
        def _(v):
            for i in range(0, n, 2):
                src, dst = pairs[i]
                p.w(v, "pe", pe_val[i])
                v.tensor_copy(dst, p.trot[i % 4][0:dst.shape[0], 0:dst.shape[1]]
                              ).then_inc(p.sem["dv"], 1)
        p.cnt["dv"] = dv0 + (n + 1) // 2

        @p.block.scalar
        def _(a):
            for i in range(1, n, 2):
                src, dst = pairs[i]
                p.w(a, "pe", pe_val[i])
                a.activation(dst, p.trot[i % 4][0:dst.shape[0], 0:dst.shape[1]],
                             AF.Copy).then_inc(p.sem["act"], 1)
        p.cnt["act"] = act0 + n // 2
        return pe_val, ev_val

    def _go(self, fn, *a):
        self._phn += 1
        if self._phn <= self.limit:
            fn(*a)

    def _emit(self):
        self._prologue()
        for l in self.layers:
            self._go(self._ph_ln, l)
            self._go(self._ph_xnT_own)
            if l == 0:
                self._go(self._ph_gate)
                self._go(self._ph_moe)
            else:
                self._go(self._ph_ffn1)
            self._go(self._ph_stages, l)
            self._go(self._ph_stab, l)
            self._go(self._ph_attn, l)
            self._go(self._ph_attn_out, l)
        self._epilogue()

    def _prologue(self):
        p = self

        @p.block.sync
        def _(s):
            for n, dst in [("ident", p.ident), ("onesb", p.onesb),
                           ("onesc", p.onesc), ("onesf", p.onesf),
                           ("hsel", p.hsel), ("me", p.mes)]:
                p.inc(s.dma_start(out=dst[:], in_=p.din[n]), "syn", 16)
            p.inc(s.dma_start(
                out=p.x[:].rearrange("p (t d) -> p t d", t=NT),
                in_=p.din["x"].rearrange("(t p) d -> p t d", p=128)), "syn", 16)

    def _epilogue(self):
        p = self

        @p.block.sync
        def _(s):
            p.w(s, "dv")
            p.w(s, "act")
            p.w(s, "pe")
            p.inc(s.dma_start(
                out=p.d_y.rearrange("(t p) d -> p t d", p=128),
                in_=p.x[:].rearrange("p (t d) -> p t d", t=NT)), "syn", 16)
            p.w(s, "syn")

    def _ph_ln(self, l):
        p = self
        st = p.stats

        @p.block.vector
        def _(v):
            p.w(v, "syn")
            p.w(v, "act")
            p.w(v, "pe")
            p.w(v, "dv")
            for t in range(NT):
                inst = v.tensor_reduce(st[:, t:t + 1],
                                       p.x[:, 1024 * t:1024 * (t + 1)], AX, OP.add)
            p.inc(inst, "dv", 1)

        @p.block.scalar
        def _(a):
            p.w(a, "dv")
            p.w(a, "act")
            for t in range(NT):
                xt = p.x[:, 1024 * t:1024 * (t + 1)]
                inst = a.activation(p.qsq[:].bitcast(F32)[:, 0:512], xt[:, 0:512],
                                    AF.Square, accum_out=st[:, 32 + t:33 + t])
                p.inc(inst, "act", 1)
                p.w(a, "act")
                inst = a.activation(p.qsq[:].bitcast(F32)[:, 512:1024],
                                    xt[:, 512:1024], AF.Square,
                                    accum_out=st[:, 48 + t:49 + t])
                p.inc(inst, "act", 1)
                p.w(a, "act")

        @p.block.vector
        def _(v):
            p.w(v, "act")
            v.tensor_scalar(st[:, 16:32], st[:, 0:16], -1.0 / D, None, OP.mult)
            inst = v.tensor_add(st[:, 64:80], st[:, 32:48], st[:, 48:64])
            p.inc(inst, "dv", 1)
            p.w(v, "dv")
            v.tensor_scalar(st[:, 64:80], st[:, 64:80], 1.0 / D, None, OP.mult)
            inst = v.tensor_mul(st[:, 80:96], st[:, 16:32], st[:, 16:32])
            p.inc(inst, "dv", 1)
            p.w(v, "dv")
            inst = v.tensor_sub(st[:, 64:80], st[:, 64:80], st[:, 80:96])
            p.inc(inst, "dv", 1)

        @p.block.scalar
        def _(a):
            p.w(a, "dv")
            inst = a.activation(st[:, 80:96], st[:, 64:80], AF.Ln, bias=np.float32(1e-5))
            p.inc(inst, "act", 1)
            p.w(a, "act")
            inst = a.activation(st[:, 64:80], st[:, 80:96], AF.Exp, scale=np.float32(-0.5))
            p.inc(inst, "act", 1)

        @p.block.vector
        def _(v):
            p.w(v, "act")
            inst = v.tensor_mul(st[:, 16:32], st[:, 16:32], st[:, 64:80])
            p.inc(inst, "dv", 1)
            p.w(v, "dv")
            for t in range(NT):
                inst = v.tensor_scalar(p.pd[:, 1024 * t:1024 * (t + 1)],
                                       p.x[:, 1024 * t:1024 * (t + 1)],
                                       st[:, 16 + t:17 + t], st[:, 64 + t:65 + t],
                                       OP.add, OP.mult)
            p.inc(inst, "dv", 1)
        self.ln_done_dv = self.cnt["dv"]

        @p.block.sync
        def _(s):
            p.w(s, "dv")
            p.inc(s.dma_start(
                out=self.dsc[("xnp", l)].rearrange("(t p) d -> p t d", p=128),
                in_=p.pd[:, 0:NT * 1024].rearrange("p (t d) -> p t d", t=NT)),
                "syn", 16)
        self.xnp_out_syn = self.cnt["syn"]

        @p.block.gpsimd
        def _(g):
            p.w(g, "syn", self.xnp_out_syn)
            g.collective_compute(
                "AllGather", OP.bypass, replica_groups=GROUPS,
                ins=[self.dsc[("xnp", l)].opt()],
                outs=[self.dsc[("xng", l)].opt()],
            ).then_inc(self.ccs[(l, "xn")])

    def _ph_xnT_own(self):
        p = self
        pairs = []
        for tt in range(NT):
            for dc in range(ND):
                src = p.pd[:, 1024 * tt + 128 * dc:1024 * tt + 128 * (dc + 1)]
                dst = p.pt[:, 2048 * dc + 128 * tt:2048 * dc + 128 * (tt + 1)]
                pairs.append((src, dst))
        self._transposes(pairs)
        self.xnT_dv = self.cnt["dv"]
        self.xnT_act = self.cnt["act"]

    def _ph_gate(self):
        p = self

        @p.block.gpsimd
        def _(g):
            p.w(g, "gpd")
            p.w(g, "syn", self.xnp_out_syn)
            p.w(g, "pe")
            p.inc(g.dma_start(out=p.gw[:].rearrange("p (dc e) -> p dc e", dc=8),
                              in_=p.din["gatew"][0:D, :]
                              .rearrange("(dc p) e -> p dc e", p=128)), "gpd", 16)
            p.inc(g.dma_start(out=p.grow[:], in_=p.din["gatew"][D:D + 1, :]),
                  "gpd", 16)
            p.inc(g.dma_start(out=p.b2s[:], in_=p.din["b2"]), "gpd", 16)
            p.w(g, "gpd")
        gate_gpd = self.cnt["gpd"]
        xnT_dv, xnT_act = self.xnT_dv, self.xnT_act

        def mk_mm(t, k, ps):
            it = k
            ps8 = ps[0:128, 0:8]
            for dc in range(ND):
                t.matmul(ps8,
                         p.pt[:, 2048 * dc + 128 * it:2048 * dc + 128 * (it + 1)],
                         p.gw[:, 8 * dc:8 * (dc + 1)], start=(dc == 0), stop=False)
            return t.matmul(ps8, p.onesb[0:1, 0:128], p.grow[0:1, :],
                            start=False, stop=True)

        def mk_ev(v, k, ps):
            return v.tensor_copy(p.gl[:, 8 * k:8 * (k + 1)], ps[0:128, 0:8])

        def pw(t):
            p.w(t, "gpd", gate_gpd)
            p.w(t, "dv", xnT_dv)
            p.w(t, "act", xnT_act)
        self._mm_groups(NT, mk_mm, mk_ev, pre_wait=pw)

        gst = p.gst

        def step_v(v, fn):
            inst = None
            for g in range(NT):
                inst = fn(v, g)
            p.inc(inst, "dv", 1)
            p.w(v, "dv")

        @p.block.vector
        def _(v):
            p.w(v, "dv")
            step_v(v, lambda v, g: v.reduce_max(gst[:, g:g + 1],
                                                p.gl[:, 8 * g:8 * g + 8], AX))
            step_v(v, lambda v, g: v.tensor_scalar(
                p.gl[:, 8 * g:8 * g + 8], p.gl[:, 8 * g:8 * g + 8],
                gst[:, g:g + 1], None, OP.subtract))
        gate_sub_dv = self.cnt["dv"]

        @p.block.scalar
        def _(a):
            p.w(a, "dv", gate_sub_dv)
            for g in range(NT):
                inst = a.activation(p.ge[:, 8 * g:8 * g + 8],
                                    p.gl[:, 8 * g:8 * g + 8], AF.Exp,
                                    accum_out=gst[:, 16 + g:17 + g])
            p.inc(inst, "act", 1)
        gate_exp_act = self.cnt["act"]

        @p.block.vector
        def _(v):
            p.w(v, "act", gate_exp_act)
            step_v(v, lambda v, g: v.reciprocal(gst[:, g:g + 1],
                                                gst[:, 16 + g:17 + g]))
            step_v(v, lambda v, g: v.tensor_scalar(
                p.ge[:, 8 * g:8 * g + 8], p.ge[:, 8 * g:8 * g + 8],
                gst[:, g:g + 1], None, OP.mult))
            step_v(v, lambda v, g: v.reduce_max(gst[:, 32 + g:33 + g],
                                                p.ge[:, 8 * g:8 * g + 8], AX))
            step_v(v, lambda v, g: v.tensor_scalar(
                p.gl[:, 8 * g:8 * g + 8], p.ge[:, 8 * g:8 * g + 8],
                gst[:, 32 + g:33 + g], None, OP.is_ge))
            step_v(v, lambda v, g: v.scalar_tensor_tensor(
                p.gm[:, 8 * g:8 * g + 8], p.gl[:, 8 * g:8 * g + 8], -1e30,
                p.ge[:, 8 * g:8 * g + 8], OP.mult, OP.add))
            step_v(v, lambda v, g: v.reduce_max(gst[:, 48 + g:49 + g],
                                                p.gm[:, 8 * g:8 * g + 8], AX))
            step_v(v, lambda v, g: v.tensor_scalar(
                p.gl[:, 8 * g:8 * g + 8], p.ge[:, 8 * g:8 * g + 8],
                gst[:, 48 + g:49 + g], None, OP.is_ge))
            step_v(v, lambda v, g: v.tensor_mul(
                p.cw[:, 8 * g:8 * g + 8], p.ge[:, 8 * g:8 * g + 8],
                p.gl[:, 8 * g:8 * g + 8]))
            inst = v.memset(p.cw9[:], 1.0)
            p.inc(inst, "dv", 1)
            p.w(v, "dv")
            step_v(v, lambda v, g: v.tensor_copy(
                p.cw9[:, 16 * g:16 * g + 8], p.cw[:, 8 * g:8 * g + 8]))
        cw9_dv = self.cnt["dv"]

        pairs = [(p.cw9[:, 16 * it:16 * it + 9],
                  p.cwT[0:9, 128 * it:128 * (it + 1)]) for it in range(NT)]

        def pw2(t):
            p.w(t, "dv", cw9_dv)
            p.w(t, "act")
        self._transposes(pairs, pre_wait=pw2)
        cwT_dv, cwT_act = self.cnt["dv"], self.cnt["act"]

        def t2_mm(t, k, ps):
            it, es = k // 2, k % 2
            return t.matmul(ps, p.cwT[0:9, 128 * it:128 * (it + 1)],
                            p.b2s[0:9, 512 * es:512 * (es + 1)],
                            start=True, stop=True)

        def t2_ev(v, k, ps):
            it, es = k // 2, k % 2
            xs = p.x[:, 1024 * it + 512 * es:1024 * it + 512 * (es + 1)]
            return v.tensor_tensor(xs, ps, xs, OP.add)

        def pw3(t):
            p.w(t, "gpd", gate_gpd)
            p.w(t, "dv", cwT_dv)
            p.w(t, "act", cwT_act)
        self._mm_groups(NT * 2, t2_mm, t2_ev, pre_wait=pw3)

    def _ph_moe(self):
        marks = {}
        for ex in range(9):
            self._moe_expert(ex, marks)

    def _moe_expert(self, ex, marks):
        p = self
        slot = ex % 2

        @p.block.gpsimd
        def _(g):
            if ex > 0:
                p.w(g, "pe", marks[("h1", ex - 1)])
            else:
                p.w(g, "syn", self.xnp_out_syn)
            p.inc(g.dma_start(
                out=p.pb[:].rearrange("p (dc f) -> p dc f", dc=8),
                in_=p.din["w1"][ex].rearrange("(dc p) f -> p dc f", p=128)),
                "gpd", 16)
            marks[("w1dma", ex)] = p.cnt["gpd"]
            p.w(g, "gpd")
            if ex > 1:
                p.w(g, "pe", marks[("w2", ex - 2)])
            p.inc(g.dma_start(
                out=p.pd[:, 8192 * slot:8192 * (slot + 1)]
                .rearrange("p (dc e) -> p dc e", dc=8),
                in_=p.din["w2"][ex].rearrange("(dc p) e -> p dc e", p=128)),
                "gpd", 16)
            marks[("w2dma", ex)] = p.cnt["gpd"]
            p.w(g, "gpd")
            if ex > 1:
                p.w(g, "dv", marks[("h1evd", ex - 2)])
                p.w(g, "act", marks[("h1eva", ex - 2)])
            p.inc(g.dma_start(
                out=p.b1s[:, 16 * slot:16 * (slot + 1)],
                in_=p.din["b1"][ex]), "gpd", 16)
            marks[("b1dma", ex)] = p.cnt["gpd"]
            p.w(g, "gpd")

        for half in range(2):
            self._moe_half(ex, half, marks)
        marks[("h1", ex)] = marks[("h1_last",)]
        marks[("w2", ex)] = marks[("w2_last",)]
        marks[("h1evd", ex)] = marks[("h1evd_last",)]
        marks[("h1eva", ex)] = marks[("h1eva_last",)]

    def _moe_half(self, ex, half, marks):
        p = self
        slot = ex % 2
        xnT_dv, xnT_act = self.xnT_dv, self.xnT_act
        w1dma = marks[("w1dma", ex)]
        w2dma = marks[("w2dma", ex)]
        b1dma = marks[("b1dma", ex)]

        def h1_mm(t, k, ps):
            fc, isg = k // 2, k % 2
            i0 = 1024 * half + 512 * isg
            inst = None
            for dc in range(ND):
                inst = t.matmul(
                    ps, p.pb[:, 2048 * dc + 128 * fc:2048 * dc + 128 * (fc + 1)],
                    p.pt[:, 2048 * dc + i0:2048 * dc + i0 + 512],
                    start=(dc == 0), stop=(dc == ND - 1))
            return inst

        def h1_ev(eng, k, ps):
            fc, isg = k // 2, k % 2
            if fc < 8:
                dst = p.pc[:, 1024 * fc + 512 * isg:1024 * fc + 512 * (isg + 1)]
                return eng.activation(dst, ps, AF.Silu,
                                      bias=p.b1s[:, 16 * slot + fc:16 * slot + fc + 1])
            dst = p.pe_[:, 1024 * (fc - 8) + 512 * isg:
                        1024 * (fc - 8) + 512 * (isg + 1)]
            return eng.tensor_scalar(dst, ps,
                                     p.b1s[:, 16 * slot + fc:16 * slot + fc + 1],
                                     None, OP.add)

        def h1_pw(t):
            p.w(t, "gpd", w1dma)
            p.w(t, "dv")
            p.w(t, "act")

        def h1_evw(eng):
            p.w(eng, "gpd", b1dma)

        self._mm_groups(32, h1_mm, h1_ev, pre_wait=h1_pw, evac_pre_wait=h1_evw,
                        evac_eng_of=lambda k: "act" if (k // 2) < 8 else "dv")
        marks[("h1_last",)] = self.cnt["pe"]
        marks[("h1evd_last",)] = self.cnt["dv"]
        marks[("h1eva_last",)] = self.cnt["act"]

        @p.block.vector
        def _(v):
            p.w(v, "dv")
            p.w(v, "act")
            for fc in range(8):
                inst = v.tensor_mul(p.pc[:, 1024 * fc:1024 * (fc + 1)],
                                    p.pc[:, 1024 * fc:1024 * (fc + 1)],
                                    p.pe_[:, 1024 * fc:1024 * (fc + 1)])
            p.inc(inst, "dv", 1)
        g_dv = self.cnt["dv"]

        def w2_mm(t, k, ps):
            itl, es = k // 2, k % 2
            inst = None
            for fc in range(ND):
                inst = t.matmul(
                    ps, p.pc[:, 1024 * fc + 128 * itl:1024 * fc + 128 * (itl + 1)],
                    p.pd[:, 8192 * slot + 1024 * fc + 512 * es:
                         8192 * slot + 1024 * fc + 512 * (es + 1)],
                    start=(fc == 0), stop=(fc == ND - 1))
            return inst

        def w2_ev(v, k, ps):
            itl, es = k // 2, k % 2
            it = 8 * half + itl
            xs = p.x[:, 1024 * it + 512 * es:1024 * it + 512 * (es + 1)]
            if ex < 8:
                return v.scalar_tensor_tensor(
                    xs, ps, p.cw[:, 8 * it + ex:8 * it + ex + 1], xs,
                    OP.mult, OP.add)
            return v.tensor_tensor(xs, ps, xs, OP.add)

        def w2_pw(t):
            p.w(t, "gpd", w2dma)
            p.w(t, "dv", g_dv)
            p.w(t, "act")
        self._mm_groups(16, w2_mm, w2_ev, pre_wait=w2_pw)
        marks[("w2_last",)] = self.cnt["pe"]

    def _ph_ffn1(self):
        p = self
        wm = {"prev_s0": None, "prev_s1": None}

        @p.block.gpsimd
        def _(g):
            p.w(g, "syn", self.xnp_out_syn)
            p.inc(g.dma_start(out=p.ffbs[:], in_=p.din["ffb"]), "gpd", 16)
            p.w(g, "gpd")
        bias_gpd = self.cnt["gpd"]

        for half in range(2):
            @p.block.gpsimd
            def _(g, half=half):
                if wm["prev_s0"] is not None:
                    p.w(g, "pe", wm["prev_s0"])
                p.inc(g.dma_start(
                    out=p.pd[:, 0:8192].rearrange("p (dc e) -> p dc e", dc=8),
                    in_=p.din["ffw"].rearrange("(dc p) e -> p dc e", p=128)),
                    "gpd", 16)
                wm[("dma", 0)] = p.cnt["gpd"]
                if half == 0:
                    p.inc(g.dma_start(
                        out=p.pd[:, 8192:16384].rearrange("p (dc e) -> p dc e", dc=8),
                        in_=p.din["ffa1"].rearrange("(dc p) e -> p dc e", p=128)),
                        "gpd", 16)
                    wm[("dma", 1)] = p.cnt["gpd"]
                p.w(g, "gpd")

            for wi in range(3):
                sl = 0 if wi != 1 else 1
                src = [p.pt, p.pc, p.pe_][wi]
                if wi == 2:
                    @p.block.gpsimd
                    def _(g):
                        p.w(g, "pe", wm[("used0",)])
                        p.inc(g.dma_start(
                            out=p.pd[:, 0:8192].rearrange("p (dc e) -> p dc e", dc=8),
                            in_=p.din["ffa2"].rearrange("(dc p) e -> p dc e", p=128)),
                            "gpd", 16)
                        wm[("dma", 0)] = p.cnt["gpd"]
                        p.w(g, "gpd")
                wdma = wm[("dma", sl)]

                def mk_mm(t, k, ps, wi=wi, sl=sl, src=src, half=half):
                    oc, isg = k // 2, k % 2
                    inst = None
                    for dc in range(ND):
                        if wi == 0:
                            rhs = src[:, 2048 * dc + 1024 * half + 512 * isg:
                                      2048 * dc + 1024 * half + 512 * (isg + 1)]
                        else:
                            rhs = src[:, 1024 * dc + 512 * isg:
                                      1024 * dc + 512 * (isg + 1)]
                        inst = t.matmul(
                            ps, p.pd[:, 8192 * sl + 1024 * dc + 128 * oc:
                                     8192 * sl + 1024 * dc + 128 * (oc + 1)],
                            rhs, start=(dc == 0), stop=(dc == ND - 1))
                    return inst

                if wi < 2:
                    dst = [p.pc, p.pe_][wi]

                    def mk_ev(v, k, ps, dst=dst):
                        oc, isg = k // 2, k % 2
                        return v.tensor_copy(
                            dst[:, 1024 * oc + 512 * isg:1024 * oc + 512 * (isg + 1)],
                            ps)
                    eng_of = lambda k: "dv"
                else:
                    def mk_ev(a, k, ps):
                        oc, isg = k // 2, k % 2
                        return a.activation(
                            p.pc[:, 1024 * oc + 512 * isg:1024 * oc + 512 * (isg + 1)],
                            ps, AF.Tanh, bias=p.ffbs[:, oc:oc + 1])
                    eng_of = lambda k: "act"

                def pw(t, wdma=wdma):
                    p.w(t, "gpd", max(wdma, bias_gpd))
                    p.w(t, "dv")
                    p.w(t, "act")
                self._mm_groups(16, mk_mm, mk_ev, pre_wait=pw, evac_eng_of=eng_of)
                if wi == 0:
                    wm[("used0",)] = self.cnt["pe"]
            wm["prev_s0"] = self.cnt["pe"]

            pairs = []
            dsts = []
            for itl in range(8):
                it = 8 * half + itl
                for oc in range(ND):
                    pairs.append((p.pc[:, 1024 * oc + 128 * itl:
                                       1024 * oc + 128 * (itl + 1)], None))
                    dsts.append(p.x[:, 1024 * it + 128 * oc:
                                    1024 * it + 128 * (oc + 1)])
            self._transpose_add(pairs, dsts)

    def _transpose_add(self, pairs, dsts):
        p = self
        n = len(pairs)
        pe0 = p.cnt["pe"]
        pe_val = [pe0 + i + 1 for i in range(n)]
        dv0 = p.cnt["dv"]
        ev_val = [dv0 + i + 1 for i in range(n)]

        @p.block.tensor
        def _(t):
            p.w(t, "dv")
            p.w(t, "act")
            for i, (src, _d) in enumerate(pairs):
                if i >= 4:
                    p.w(t, "dv", ev_val[i - 4])
                inst = t.transpose(p.trot[i % 4], src, p.ident[:])
                inst.then_inc(p.sem["pe"], 1)
        p.cnt["pe"] = pe_val[-1]

        @p.block.vector
        def _(v):
            for i in range(n):
                p.w(v, "pe", pe_val[i])
                xs = dsts[i]
                v.tensor_tensor(xs, p.trot[i % 4], xs, OP.add).then_inc(
                    p.sem["dv"], 1)
        p.cnt["dv"] = dv0 + n

    def _ph_stages(self, l):
        p = self
        L0 = (l == 0)

        @p.block.gpsimd
        def _(g):
            p.w(g, "pe")
            for wi, wn in enumerate(["wq", "wk", "wv"]):
                p.inc(g.dma_start(
                    out=p.wqkv[:, 2048 * wi:2048 * (wi + 1)]
                    .rearrange("p (dc e) -> p dc e", dc=8),
                    in_=p.din[wn + ("0" if L0 else "1")]
                    .rearrange("(dc p) e -> p dc e", p=128)), "gpd", 16)
            if L0:
                p.inc(g.dma_start(out=p.bqs[:], in_=p.din["bq0"]), "gpd", 16)
                p.inc(g.dma_start(out=p.bks[:], in_=p.din["bk0"]), "gpd", 16)
                p.inc(g.dma_start(out=p.bvrow[:], in_=p.din["bv0"]), "gpd", 16)
            p.w(g, "gpd")
        wdma = self.cnt["gpd"]

        @p.block.vector
        def _(v):
            p.w(v, "dv")
            v.memset(p.qn[:, 0:1], 0.0)
            inst = v.memset(p.kn[:, 0:1], 0.0)
            p.inc(inst, "dv", 1)

        for s in range(self.stage_count):
            self._stage(l, s, wdma)

    def _stage(self, l, s, wdma):
        p = self
        L0 = (l == 0)
        xng = self.dsc[("xng", l)]

        pairs = []
        for tt in range(NT):
            for dc in range(ND):
                src = p.roll[tt % 2][:, 128 * dc:128 * (dc + 1)]
                dst = p.pt[:, 2048 * dc + 128 * tt:2048 * dc + 128 * (tt + 1)]
                pairs.append((src, dst))
        n = len(pairs)
        pe0 = p.cnt["pe"]
        pe_val = [pe0 + i + 1 for i in range(n)]
        dv0, act0 = p.cnt["dv"], p.cnt["act"]
        ev_val = [("dv", dv0 + i // 2 + 1) if i % 2 == 0 else
                  ("act", act0 + i // 2 + 1) for i in range(n)]
        syn0 = p.cnt["syn"]
        roll_syn = [syn0 + 16 * (tt + 1) for tt in range(NT)]

        pet0 = p.cnt["pet"]

        @p.block.sync
        def _(sy):
            sy.wait_ge(self.ccs[(l, "xn")], 1)
            for tt in range(NT):
                if tt >= 2:
                    p.w(sy, "pet", pet0 + tt - 1)
                inst = sy.dma_start(
                    out=p.roll[tt % 2],
                    in_=xng[2048 * s + 128 * tt:2048 * s + 128 * (tt + 1), :])
                p.inc(inst, "syn", 16)
        assert p.cnt["syn"] == roll_syn[-1]

        @p.block.tensor
        def _(t):
            p.w(t, "dv")
            p.w(t, "act")
            for i, (src, dst) in enumerate(pairs):
                tt = i // ND
                if i % ND == 0:
                    p.w(t, "syn", roll_syn[tt])
                if i >= 4:
                    e, v = ev_val[i - 4]
                    p.w(t, e, v)
                inst = t.transpose(p.trot[i % 4], src, p.ident[:])
                inst.then_inc(p.sem["pe"], 1)
                if i % ND == ND - 1:
                    t.nop().then_inc(p.sem["pet"], 1)
                    p.cnt["pet"] += 1
        p.cnt["pe"] = pe_val[-1]

        @p.block.vector
        def _(v):
            for i in range(0, n, 2):
                p.w(v, "pe", pe_val[i])
                v.tensor_copy(pairs[i][1], p.trot[i % 4]).then_inc(p.sem["dv"], 1)
        p.cnt["dv"] = dv0 + (n + 1) // 2

        @p.block.scalar
        def _(a):
            for i in range(1, n, 2):
                p.w(a, "pe", pe_val[i])
                a.activation(pairs[i][1], p.trot[i % 4], AF.Copy).then_inc(
                    p.sem["act"], 1)
        p.cnt["act"] = act0 + n // 2
        stage_dv, stage_act = p.cnt["dv"], p.cnt["act"]

        if self.stage_sub < 2:
            return
        for which in range(min(2, self.stage_sub - 1)):
            wofs = 2048 * which
            bias = (p.bqs if which == 0 else p.bks) if L0 else None
            nrun = p.qn if which == 0 else p.kn

            def qk_mm(t, k, ps, wofs=wofs):
                isg, cc = k // 2, k % 2
                inst = None
                for dc in range(ND):
                    inst = t.matmul(
                        ps, p.wqkv[:, wofs + 256 * dc + 128 * cc:
                                   wofs + 256 * dc + 128 * (cc + 1)],
                        p.pt[:, 2048 * dc + 512 * isg:2048 * dc + 512 * (isg + 1)],
                        start=(dc == 0), stop=(dc == ND - 1))
                return inst

            qsyn0 = p.cnt["syn"]

            def qk_ev(v, k, ps, which=which, bias=bias, s=s, qsyn0=qsyn0):
                isg, cc = k // 2, k % 2
                if which == 0:
                    if k >= 2:
                        p.w(v, "syn", qsyn0 + 16 * (k - 1))
                    dst = p.qbuf[:, 512 * (k % 2):512 * (k % 2 + 1)]
                else:
                    buf = p.pc if cc == 0 else p.pe_
                    dst = buf[:, 2048 * s + 512 * isg:2048 * s + 512 * (isg + 1)]
                if bias is not None:
                    return v.tensor_scalar(dst, ps, bias[:, cc:cc + 1],
                                           None, OP.add)
                return v.tensor_copy(dst, ps)

            def qk_evw(eng):
                p.w(eng, "gpd", wdma)

            do_sq = bool(self.qk_parts & 4)
            act0b = p.cnt["act"]
            sq_act = [act0b + k + 1 for k in range(8)]

            def qk_pw(t):
                p.w(t, "gpd", wdma)
                p.w(t, "dv", p.cnt["dv"])
                p.w(t, "act", p.cnt["act"])
            mm_pe, _ev = self._mm_groups(
                8, qk_mm, qk_ev, pre_wait=qk_pw, evac_pre_wait=qk_evw,
                extra_release=(lambda k: ("act", sq_act[k])) if do_sq else None)
            if which == 0 and (self.qk_parts & 2):
                qres = self.dsc[("qres", l)]

                @p.block.sync
                def _(sy, _ev=_ev, s=s, qres=qres, qsyn0=qsyn0):
                    for k in range(8):
                        isg, cc = k // 2, k % 2
                        p.w(sy, "dv", _ev[k][1])
                        inst = sy.dma_start(
                            out=qres[128 * cc:128 * (cc + 1),
                                     2048 * s + 512 * isg:2048 * s + 512 * (isg + 1)],
                            in_=p.qbuf[:, 512 * (k % 2):512 * (k % 2 + 1)])
                        p.inc(inst, "syn", 16)
                assert p.cnt["syn"] == qsyn0 + 16 * 8

            if not do_sq:
                continue
            pe_hs0 = p.cnt["pe"]
            hs_pe = [pe_hs0 + k + 1 for k in range(8)]
            dv0b = p.cnt["dv"]

            @p.block.scalar
            def _(a, mm_pe=mm_pe, bias=bias, hs_pe=hs_pe, _ev=_ev):
                for k in range(8):
                    cc = k % 2
                    p.w(a, "pe", mm_pe[k])
                    p.w(a, _ev[k][0], _ev[k][1])
                    if k >= 4:
                        p.w(a, "pe", hs_pe[k - 4])
                    dst = p.qsq[:, 512 * (k % 4):512 * (k % 4 + 1)]
                    if bias is not None:
                        inst = a.activation(dst, p.rot[k % 4], AF.Square,
                                            bias=bias[:, cc:cc + 1])
                    else:
                        inst = a.activation(dst, p.rot[k % 4], AF.Square)
                    inst.then_inc(p.sem["act"], 1)
            p.cnt["act"] = sq_act[-1]

            @p.block.tensor
            def _(t, sq_act=sq_act):
                p.w(t, "dv", dv0b)
                for k in range(8):
                    cc = k % 2
                    p.w(t, "act", sq_act[k])
                    if k >= 2 and cc == 0:
                        p.w(t, "dv", dv0b + 2 * (k // 2))
                    inst = t.matmul(p.psm[0:4, :], p.hsel[:, 4 * cc:4 * (cc + 1)],
                                    p.qsq[:, 512 * (k % 4):512 * (k % 4 + 1)],
                                    start=(cc == 0), stop=(cc == 1))
                    inst.then_inc(p.sem["pe"], 1)
            p.cnt["pe"] = hs_pe[-1]

            @p.block.vector
            def _(v, nrun=nrun, hs_pe=hs_pe):
                for pair in range(4):
                    p.w(v, "pe", hs_pe[2 * pair + 1])
                    inst = v.reduce_max(nrun[0:4, 1:2], p.psm[0:4, :], AX)
                    p.inc(inst, "dv", 1)
                    p.w(v, "dv")
                    inst = v.tensor_max(nrun[0:4, 0:1], nrun[0:4, 0:1],
                                        nrun[0:4, 1:2])
                    p.inc(inst, "dv", 1)
                    p.w(v, "dv")
            assert p.cnt["dv"] == dv0b + 8

        if self.stage_sub < 4:
            return
        def v_mm(t, k, ps):
            tt = k
            ps256 = ps[0:128, 0:256]
            inst = None
            for dc in range(ND):
                inst = t.matmul(
                    ps256, p.pt[:, 2048 * dc + 128 * tt:2048 * dc + 128 * (tt + 1)],
                    p.wqkv[:, 4096 + 256 * dc:4096 + 256 * (dc + 1)],
                    start=(dc == 0), stop=(dc == ND - 1 and not L0))
            if L0:
                inst = t.matmul(ps256, p.onesb[0:1, 0:128], p.bvrow[0:1, :],
                                start=False, stop=True)
            return inst

        def v_ev(a, k, ps):
            jt = 16 * s + k
            return a.activation(p.pd[:, 256 * jt:256 * (jt + 1)],
                                ps[0:128, 0:256], AF.Copy)

        def v_pw(t):
            p.w(t, "gpd", wdma)
            p.w(t, "dv", p.cnt["dv"])
            p.w(t, "act", p.cnt["act"])
        self._mm_groups(NT, v_mm, v_ev, pre_wait=v_pw,
                        evac_eng_of=lambda k: "act")

    def _ph_stab(self, l):
        p = self
        nb = self.dsc[("nb", l)]

        if l == 0:
            @p.block.sync
            def _(sy):
                p.w(sy, "pe")
                p.w(sy, "act")
                p.inc(sy.dma_start(
                    out=p.msk.rearrange("p (m i) -> p m i", m=4),
                    in_=p.din["masks"].rearrange("m p i -> p m i")), "syn", 16)

        @p.block.sync
        def _(sy):
            p.w(sy, "dv")
            p.inc(sy.dma_start(out=nb[0:4, :], in_=p.qn[0:4, 0:1]), "syn", 16)
            p.inc(sy.dma_start(out=nb[4:8, :], in_=p.kn[0:4, 0:1]), "syn", 16)
            p.w(sy, "syn")
            for tq in range(2):
                p.inc(sy.dma_start(
                    out=p.nb2[0:2, 2 * tq:2 * tq + 2],
                    in_=nb[4 * tq:4 * tq + 4, :]
                    .rearrange("(pp two) one -> pp (two one)", pp=2),
                    allow_slow_non_contiguous=True), "syn", 16)

        @p.block.vector
        def _(v):
            p.w(v, "syn")
            v.reduce_max(p.nb2[0:2, 4:5], p.nb2[0:2, 0:2], AX)
            inst = v.reduce_max(p.nb2[0:2, 5:6], p.nb2[0:2, 2:4], AX)
            p.inc(inst, "dv", 1)
            p.w(v, "dv")
            inst = v.tensor_mul(p.nb2[0:2, 6:7], p.nb2[0:2, 4:5], p.nb2[0:2, 5:6])
            p.inc(inst, "dv", 1)

        @p.block.scalar
        def _(a):
            p.w(a, "dv")
            inst = a.activation(p.nb2[0:2, 7:8], p.nb2[0:2, 6:7], AF.Ln, bias=np.float32(1e-30))
            p.inc(inst, "act", 1)
            p.w(a, "act")
            inst = a.activation(p.nb2[0:2, 6:7], p.nb2[0:2, 7:8], AF.Exp,
                                scale=np.float32(0.5), bias=np.float32(-np.log(8.0)))
            p.inc(inst, "act", 1)

        @p.block.vector
        def _(v):
            p.w(v, "act")
            inst = v.tensor_scalar(p.ncp[0:2, 0:1], p.nb2[0:2, 6:7], -1.0,
                                   None, OP.mult)
            p.inc(inst, "dv", 1)

        tp = []
        @p.block.tensor
        def _(t):
            p.w(t, "dv")
            p.w(t, "act")
            inst = t.transpose(p.trot[0][0:1, 0:2], p.ncp[0:2, 0:1],
                               p.ident[0:2, 0:2])
            tp.append(p.inc(inst, "pe", 1))

        @p.block.vector
        def _(v):
            p.w(v, "pe", tp[0])
            inst = v.tensor_copy(p.ncp[0:1, 0:2], p.trot[0][0:1, 0:2])
            p.inc(inst, "dv", 1)

        bc = []
        @p.block.tensor
        def _(t):
            p.w(t, "dv")
            p.w(t, "act")
            inst = t.matmul(p.pg[0:128, 0:2], p.onesb[0:1, 0:128],
                            p.ncp[0:1, 0:2], start=True, stop=True)
            bc.append(p.inc(inst, "pe", 1))

        @p.block.vector
        def _(v):
            p.w(v, "pe", bc[0])
            inst = v.tensor_copy(p.cb[:, 0:2], p.pg[0:128, 0:2])
            p.inc(inst, "dv", 1)
        self.cb_dv = self.cnt["dv"]

    def _ph_attn(self, l):
        p = self
        L0 = (l == 0)
        prev_act = self.cnt["act"]
        prev_fin = self.cnt["dv"]
        self._pet0 = self.cnt["pet"]
        u = 0
        for pp in range(2):
            for ib in range(NIB):
                tiles = _l0_tiles(ib) if L0 else [(jt, None) for jt in range(NJT)]
                prev_act, prev_fin = self._attn_unit(l, u, pp, ib, tiles,
                                                     prev_act, prev_fin)
                u += 1
        self.attn_done = dict(act=prev_act, dv=prev_fin, pe=self.cnt["pe"])

    def _attn_unit(self, l, u, pp, ib, tiles, prev_act, prev_fin):
        p = self
        n = len(tiles)
        E = [p.e0, p.e1]
        psS = [p.ps0, p.ps1]
        kres = [p.pc, p.pe_][pp]
        cb_dv = self.cb_dv
        qres = self.dsc[("qres", l)]

        @p.block.sync
        def _(sy):
            if u >= 2:
                p.w(sy, "pet", self._pet0 + u - 1)
            else:
                p.w(sy, "pet")
            inst = sy.dma_start(
                out=p.qbuf[:, 512 * (u % 2):512 * (u % 2 + 1)],
                in_=qres[128 * pp:128 * (pp + 1), 512 * ib:512 * (ib + 1)])
            p.inc(inst, "syn", 16)
        q_syn = p.cnt["syn"]
        qb = p.qbuf[:, 512 * (u % 2):512 * (u % 2 + 1)]

        act0 = p.cnt["act"]
        act_ = [act0 + i + 1 for i in range(n)]
        dv0 = p.cnt["dv"]
        maskdv = {}
        dvk = dv0
        for i, (jt, m) in enumerate(tiles):
            if m is not None:
                dvk += 1
                maskdv[i] = dvk
        peS = [None] * n
        peAVS = [None] * n
        pe = p.cnt["pe"]
        for i in range(n):
            pe += 1
            peS[i] = pe
            if i >= 1:
                pe += 1
                peAVS[i - 1] = pe
        pe += 1
        peAVS[n - 1] = pe

        def emit_av(t, i):
            jt = tiles[i][0]
            first, last = (i == 0), (i == n - 1)
            t.wait_ge(p.sem["act"], act_[i])
            t.matmul(p.pav[0:64, :], p.pd[:, 256 * jt + 128 * pp:
                                          256 * jt + 128 * pp + 64],
                     E[i % 2][:, 0:512], start=first, stop=last,
                     tile_position=(0, 0))
            t.matmul(p.pav[64:128, :], p.pd[:, 256 * jt + 128 * pp + 64:
                                            256 * jt + 128 * pp + 128],
                     E[i % 2][:, 512:1024], start=first, stop=last,
                     tile_position=(0, 64))
            t.matmul(p.psm[0:1, :], p.onesc[:, 0:1], E[i % 2][:, 0:512],
                     start=first, stop=last, tile_position=(0, 0))
            inst = t.matmul(p.psm[64:65, :], p.onesc[:, 0:1],
                            E[i % 2][:, 512:1024], start=first, stop=last,
                            tile_position=(0, 64))
            inst.then_inc(p.sem["pe"], 1)

        @p.block.tensor
        def _(t):
            p.w(t, "dv", prev_fin)
            p.w(t, "act", prev_act)
            p.w(t, "dv", cb_dv)
            p.w(t, "syn", q_syn)
            for i, (jt, m) in enumerate(tiles):
                if i >= 2:
                    p.w(t, "act", act_[i - 2])
                t.matmul(psS[i % 2][:, 0:512], kres[0:64, 128 * jt:128 * (jt + 1)],
                         qb[0:64, :], start=True, stop=True, tile_position=(0, 0))
                inst = t.matmul(
                    psS[i % 2][:, 512:1024], kres[64:128, 128 * jt:128 * (jt + 1)],
                    qb[64:128, :], start=True, stop=True, tile_position=(64, 0))
                inst.then_inc(p.sem["pe"], 1)
                if i >= 1:
                    emit_av(t, i - 1)
            emit_av(t, n - 1)
            t.nop().then_inc(p.sem["pet"], 1)
            p.cnt["pet"] += 1
        p.cnt["pe"] = pe

        @p.block.vector
        def _(v):
            for i, (jt, m) in enumerate(tiles):
                if m is None:
                    continue
                p.w(v, "pe", peS[i])
                v.tensor_tensor(psS[i % 2][:, 0:512], psS[i % 2][:, 0:512],
                                p.msk[:, 512 * m:512 * (m + 1)], OP.add)
                inst = v.tensor_tensor(psS[i % 2][:, 512:1024],
                                       psS[i % 2][:, 512:1024],
                                       p.msk[:, 512 * m:512 * (m + 1)], OP.add)
                p.inc(inst, "dv", 1)
        assert p.cnt["dv"] == dvk

        @p.block.scalar
        def _(a):
            p.w(a, "dv", cb_dv)
            for i, (jt, m) in enumerate(tiles):
                if m is not None:
                    p.w(a, "dv", maskdv[i])
                else:
                    p.w(a, "pe", peS[i])
                if i >= 2:
                    p.w(a, "pe", peAVS[i - 2])
                inst = a.activation(E[i % 2][:], psS[i % 2][:], AF.Exp,
                                    bias=p.cb[:, pp:pp + 1], scale=np.float32(0.125))
                inst.then_inc(p.sem["act"], 1)
        p.cnt["act"] = act_[-1]

        @p.block.vector
        def _(v):
            p.w(v, "pe", peAVS[n - 1])
            v.reciprocal(p.rbuf[0:1, 0:512], p.psm[0:1, :])
            inst = v.reciprocal(p.rbuf[64:65, 0:512], p.psm[64:65, :])
            p.inc(inst, "dv", 1)
        dvrec = p.cnt["dv"]

        rpe = []
        @p.block.tensor
        def _(t):
            p.w(t, "dv", dvrec)
            t.matmul(p.pr[0:64, :], p.onesf[0:1, 0:64], p.rbuf[0:1, 0:512],
                     start=True, stop=True, tile_position=(0, 0))
            inst = t.matmul(p.pr[64:128, :], p.onesf[64:65, 0:64],
                            p.rbuf[64:65, 0:512], start=True, stop=True,
                            tile_position=(64, 64))
            rpe.append(p.inc(inst, "pe", 1))

        @p.block.vector
        def _(v):
            p.w(v, "pe", rpe[0])
            inst = v.tensor_copy(p.rrep[:], p.pr[:])
            p.inc(inst, "dv", 1)
            p.w(v, "dv")
            inst = v.tensor_tensor(
                p.pt[:, 8192 * pp + 512 * ib:8192 * pp + 512 * (ib + 1)],
                p.pav[:], p.rrep[:], OP.mult)
            p.inc(inst, "dv", 1)
        return p.cnt["act"], p.cnt["dv"]

    def _ph_attn_out(self, l):
        p = self
        L0 = (l == 0)
        atp, atg = self.dsc[("atp", l)], self.dsc[("atg", l)]

        @p.block.sync
        def _(sy):
            p.w(sy, "dv", self.attn_done["dv"])
            for pp in range(2):
                p.inc(sy.dma_start(out=atp[128 * pp:128 * (pp + 1), :],
                                   in_=p.pt[:, 8192 * pp:8192 * (pp + 1)]),
                      "syn", 16)
        atp_syn = self.cnt["syn"]

        @p.block.gpsimd
        def _(g):
            p.w(g, "syn", atp_syn)
            g.collective_compute(
                "AllGather", OP.bypass, replica_groups=GROUPS,
                ins=[atp.opt()], outs=[atg.opt()],
            ).then_inc(self.ccs[(l, "at")])
            g.wait_ge(self.ccs[(l, "at")], 1)
            p.w(g, "pe", self.attn_done["pe"])
            mev = g.value_load(p.mes[0:1, 0:1])
            gv = atg.rearrange("(dc pq) (r i) -> dc pq r i", pq=128, r=4)
            for dc in range(ND):
                inst = g.dma_start(out=p.pb[:, 2048 * dc:2048 * (dc + 1)],
                                   in_=gv[dc, :, mev, :])
                p.inc(inst, "gpd", 16)
            p.w(g, "gpd")
        ao_gpd = self.cnt["gpd"]

        if L0:
            @p.block.gpsimd
            def _(g):
                p.w(g, "pe", self.attn_done["pe"])
                p.inc(g.dma_start(
                    out=p.pc[:].rearrange("p (dc e) -> p dc e", dc=8),
                    in_=p.din["wo0"][0:D, :].rearrange("(dc p) e -> p dc e", p=128)),
                    "gpd", 16)
                p.inc(g.dma_start(out=p.worow[:], in_=p.din["wo0"][D:D + 1, :]),
                      "gpd", 16)
                p.w(g, "gpd")
            wo_gpd = self.cnt["gpd"]

            def o_mm(t, k, ps):
                it, es = k // 2, k % 2
                for dc in range(ND):
                    t.matmul(ps,
                             p.pb[:, 2048 * dc + 128 * it:2048 * dc + 128 * (it + 1)],
                             p.pc[:, 1024 * dc + 512 * es:1024 * dc + 512 * (es + 1)],
                             start=(dc == 0), stop=False)
                return t.matmul(ps, p.onesb[0:1, 0:128],
                                p.worow[0:1, 512 * es:512 * (es + 1)],
                                start=False, stop=True)

            def o_ev(v, k, ps):
                it, es = k // 2, k % 2
                xs = p.x[:, 1024 * it + 512 * es:1024 * it + 512 * (es + 1)]
                return v.tensor_tensor(xs, ps, xs, OP.add)

            def o_pw(t):
                p.w(t, "gpd", wo_gpd)
                p.w(t, "dv")
                p.w(t, "act")
            self._mm_groups(NT * 2, o_mm, o_ev, pre_wait=o_pw)
        else:
            pairs = []
            dsts = []
            for it in range(NT):
                for dc in range(ND):
                    pairs.append((p.pb[:, 2048 * dc + 128 * it:
                                       2048 * dc + 128 * (it + 1)], None))
                    dsts.append(p.x[:, 1024 * it + 128 * dc:
                                    1024 * it + 128 * (dc + 1)])

            @p.block.tensor
            def _(t):
                p.w(t, "gpd", ao_gpd)
            self._transpose_add(pairs, dsts)


_PROG_CACHE = {}


def _get_prog(layers):
    key = tuple(layers)
    if key not in _PROG_CACHE:
        _PROG_CACHE[key] = Prog(layers)
    return _PROG_CACHE[key]


def run_layers(layers, per_core):
    prog = _get_prog(layers)
    res = run_bass_kernel_spmd(prog.nc, per_core, core_ids=list(range(8)))
    return [np.asarray(r["y"], dtype=np.float32) for r in res.results]


def assemble(ys):
    out = np.empty((B, S, D), dtype=np.float32)
    for c in range(8):
        b, me = c // 4, c % 4
        out[b, 2048 * me:2048 * (me + 1)] = ys[c]
    return _unperm(out)


def kernel(**inputs):
    per_core = host_prep(inputs)
    ys = run_layers([0], per_core)
    for c in range(8):
        per_core[c]["x"] = ys[c]
    ys = run_layers([1], per_core)
    return assemble(ys)


# version 02 (final)
# speedup vs baseline: 1.0061x; 1.0061x over previous
import numpy as np
import ml_dtypes
from contextlib import ExitStack

import concourse.bass as bass
import concourse.mybir as mybir
from concourse.bass_utils import run_bass_kernel_spmd

F32 = mybir.dt.float32
BF16 = mybir.dt.bfloat16
I32 = mybir.dt.int32
AF = mybir.ActivationFunctionType
OP = mybir.AluOpType
AX = mybir.AxisListType.X
BF16NP = ml_dtypes.bfloat16

B = 2
S = 8192
D = 1024
NTOK = 2048
NT = 16
ND = 8
NIB = 16
NJT = 64
NSTG = 4
FFN = 2048
NEG = -1e30
GROUPS = [[0, 1, 2, 3], [4, 5, 6, 7]]


def _perm(x):
    return np.ascontiguousarray(
        x.reshape(B, 16, 512, D).transpose(0, 2, 1, 3).reshape(B, S, D))


def _unperm(x):
    return np.ascontiguousarray(
        x.reshape(B, 512, 16, D).transpose(0, 2, 1, 3).reshape(B, S, D))


def _c(a):
    return np.ascontiguousarray(np.asarray(a), dtype=np.float32)


def _masks_host():
    m = np.arange(4)[:, None, None] * 8 + (np.arange(128) // 16)[None, :, None]
    ii = (np.arange(512) // 16)[None, None, :]
    return np.where(m <= ii, 0.0, NEG).astype(ml_dtypes.bfloat16)


def host_prep(inputs):
    xp = _perm(_c(inputs["L0_attn_x"]))
    wq0 = _c(inputs["L0_attn_Wq"]).T
    wk0 = _c(inputs["L0_attn_Wk"]).T
    wv0 = _c(inputs["L0_attn_Wv"]).T
    wo0a = np.concatenate([_c(inputs["L0_attn_Wo"]).T,
                           _c(inputs["L0_attn_bo"])[None, :]], axis=0)
    bq0 = _c(inputs["L0_attn_bq"])
    bk0 = _c(inputs["L0_attn_bk"])
    bv0 = _c(inputs["L0_attn_bv"])
    gate_aug = np.concatenate([_c(inputs["L0_ffn_gate_w"]),
                               _c(inputs["L0_ffn_gate_b"])[None, :]], axis=0)
    w1_all = np.concatenate([_c(inputs["L0_ffn_rw1"]), _c(inputs["L0_ffn_sw1"])], 0)
    b1_all = np.concatenate([_c(inputs["L0_ffn_rb1"]), _c(inputs["L0_ffn_sb1"])], 0)
    w2_all = np.concatenate([_c(inputs["L0_ffn_rw2"]), _c(inputs["L0_ffn_sw2"])], 0)
    b2_all = np.concatenate([_c(inputs["L0_ffn_rb2"]), _c(inputs["L0_ffn_sb2"])], 0)
    wq1 = _c(inputs["L1_attn_Wq"]).T
    wk1 = _c(inputs["L1_attn_Wk"]).T
    wv1 = _c(inputs["L1_attn_Wv"]).T
    ident = np.eye(128, dtype=BF16NP)
    onesb = np.ones((1, 128), dtype=BF16NP)
    onesc = np.ones((128, 1), dtype=BF16NP)
    onesf = np.ones((128, 64), dtype=np.float32)
    hsel = np.zeros((128, 8), dtype=BF16NP)
    for cc in range(2):
        for hl in range(2):
            hsel[64 * hl:64 * hl + 64, 4 * cc + (2 * cc + hl)] = 1.0
    masks = _masks_host()

    per_core = []
    for c in range(8):
        b, me = c // 4, c % 4
        sl = slice(256 * me, 256 * me + 256)
        per_core.append({
            "x": np.ascontiguousarray(xp[b, 2048 * me:2048 * (me + 1)]),
            "me": np.array([[me]], dtype=np.int32),
            "ident": ident, "onesb": onesb, "onesc": onesc, "onesf": onesf,
            "hsel": hsel, "masks": masks,
            "wq0": np.ascontiguousarray(wq0[:, sl]),
            "wk0": np.ascontiguousarray(wk0[:, sl]),
            "wv0": np.ascontiguousarray(wv0[:, sl]),
            "bq0": np.ascontiguousarray(bq0[sl].reshape(2, 128).T),
            "bk0": np.ascontiguousarray(bk0[sl].reshape(2, 128).T),
            "bv0": np.ascontiguousarray(bv0[None, sl]),
            "wo0": wo0a, "gatew": gate_aug,
            "w1": w1_all,
            "b1": np.ascontiguousarray(
                b1_all.reshape(9, 16, 128).transpose(0, 2, 1)),
            "w2": w2_all, "b2": b2_all,
            "wq1": np.ascontiguousarray(wq1[:, sl]),
            "wk1": np.ascontiguousarray(wk1[:, sl]),
            "wv1": np.ascontiguousarray(wv1[:, sl]),
            "ffw": _c(inputs["L1_ffn_weight"]),
            "ffa1": _c(inputs["L1_ffn_aux_weight1"]),
            "ffa2": _c(inputs["L1_ffn_aux_weight2"]),
            "ffb": np.ascontiguousarray(
                _c(inputs["L1_ffn_bias"]).reshape(8, 128).T),
        })
    return per_core


def _l0_tiles(ib):
    out = []
    for jt in range(NJT):
        d = 8 * jt - 32 * ib
        if d <= -8:
            out.append((jt, None))
        elif d <= 24:
            out.append((jt, d // 8))
    return out


INPUT_SPECS = [
    ("x", [NTOK, D], F32), ("me", [1, 1], I32),
    ("ident", [128, 128], BF16), ("onesb", [1, 128], BF16),
    ("onesc", [128, 1], BF16), ("onesf", [128, 64], F32),
    ("hsel", [128, 8], BF16), ("masks", [4, 128, 512], BF16),
    ("wq0", [D, 256], F32), ("wk0", [D, 256], F32), ("wv0", [D, 256], F32),
    ("bq0", [128, 2], F32), ("bk0", [128, 2], F32), ("bv0", [1, 256], F32),
    ("wo0", [D + 1, D], F32), ("gatew", [D + 1, 8], F32),
    ("w1", [9, D, FFN], F32), ("b1", [9, 128, 16], F32),
    ("w2", [9, D, D], F32), ("b2", [9, D], F32),
    ("wq1", [D, 256], F32), ("wk1", [D, 256], F32), ("wv1", [D, 256], F32),
    ("ffw", [D, D], F32), ("ffa1", [D, D], F32), ("ffa2", [D, D], F32),
    ("ffb", [128, 8], F32),
]


class Prog:
    def __init__(self, layers, limit=None, stage_sub=4, stage_count=NSTG,
                 qk_parts=15):
        self.limit = limit if limit is not None else 99
        self.stage_sub = stage_sub
        self.stage_count = stage_count
        self.qk_parts = qk_parts
        self._phn = 0
        self.layers = list(layers)
        nc = bass.Bass("TRN2", target_bir_lowering=False)
        self.nc = nc
        self.cnt = {}
        self._ctx = ExitStack()
        self._dram()
        self._sbuf()
        self._psum()
        self._sems()
        self.block = self._ctx.enter_context(nc.Block())
        self._emit()
        self._ctx.close()

    def _dram(self):
        nc = self.nc
        self.din = {}
        for n, sh, dt in INPUT_SPECS:
            self.din[n] = nc.dram_tensor(n, sh, dt, kind="ExternalInput").ap()
        self.d_y = nc.dram_tensor("y", [NTOK, D], F32, kind="ExternalOutput").ap()
        self.dsc = {}
        for l in self.layers:
            self.dsc[("xnp", l)] = nc.dram_tensor(f"xnp{l}", [NTOK, D], BF16).ap()
            self.dsc[("xng", l)] = nc.dram_tensor(f"xng{l}", [S, D], BF16).ap()
            self.dsc[("atp", l)] = nc.dram_tensor(f"atp{l}", [256, S], BF16).ap()
            self.dsc[("qres", l)] = nc.dram_tensor(f"qres{l}", [256, S], BF16).ap()
            self.dsc[("atg", l)] = nc.dram_tensor(f"atg{l}", [D, S], BF16).ap()
            self.dsc[("nb", l)] = nc.dram_tensor(f"nb{l}", [8, 1], F32).ap()

    def _sbuf(self):
        E = self._ctx.enter_context
        sb = lambda n, sh, dt: E(self.nc.sbuf_tensor(n, sh, dt))
        self.x = sb("xres", [128, NT * D], F32)
        self.pt = sb("pt", [128, ND * 2048], BF16)
        self.pb = sb("pb", [128, 16384], BF16)
        self.pc = sb("pc", [128, 8192], BF16)
        self.pe_ = sb("pe_", [128, 8192], BF16)
        self.pd = sb("pd", [128, 16384], BF16)
        self.e0 = sb("e0", [128, 1024], BF16)
        self.e1 = sb("e1", [128, 1024], BF16)
        self.qbuf = sb("qbuf", [128, 1024], BF16)
        self.ident = sb("identS", [128, 128], BF16)
        self.onesb = sb("onesbS", [1, 128], BF16)
        self.onesc = sb("onescS", [128, 1], BF16)
        self.onesf = sb("onesfS", [128, 64], F32)
        self.hsel = sb("hselS", [128, 8], BF16)
        self.mes = sb("mes", [1, 1], I32)
        self.bqs = sb("bqs", [128, 2], F32)
        self.bks = sb("bks", [128, 2], F32)
        self.bvrow = sb("bvrow", [1, 256], BF16)
        self.b1s = sb("b1s", [128, 32], F32)
        self.ffbs = sb("ffbs", [128, 8], F32)
        self.rbuf = sb("rbuf", [128, 512], F32)
        self.rrep = sb("rrep", [128, 512], F32)
        self.qn = sb("qn", [4, 2], F32)
        self.kn = sb("kn", [4, 2], F32)
        self.nb2 = sb("nb2", [2, 8], F32)
        self.cb = sb("cb", [128, 2], F32)
        self.ncp = sb("ncp", [2, 2], BF16)
        self.qsq = sb("qsq", [128, 4 * 512], BF16)
        self.stats = self.rbuf[:, 384:480]
        self.b2s = self.pd[0:9, 8192:9216]
        self.gw = self.pd[:, 9216:9280]
        self.grow = self.pd[0:1, 9280:9288]
        self.wqkv = self.pb[:, 0:6144]
        self.msk = self.qsq[:, 0:2048]
        self.cwT = self.qsq[0:9, 0:2048]
        self.gm = self.qsq[:].bitcast(F32)[:, 0:128]
        self.gst = self.stats[:, 0:64]
        self.gl = self.rbuf[:, 0:128]
        self.ge = self.rbuf[:, 128:256]
        self.cw = self.rbuf[:, 256:384]
        self.cw9 = self.rrep[:].bitcast(BF16)[:, 0:256]
        self.worow = self.e0[0:1, 0:1024]
        self.roll = [self.e0[:], self.e1[:]]

    def _psum(self):
        E = self._ctx.enter_context
        ps = lambda n, sh: E(self.nc.psum_tensor(n, sh, F32))
        self.ps0 = ps("ps0", [128, 1024])
        self.ps1 = ps("ps1", [128, 1024])
        self.pav = ps("pav", [128, 512])
        self.psm = ps("psm", [128, 512])
        self.pr = ps("pr", [128, 512])
        self.pg = ps("pg", [128, 512])
        self.rot = [self.ps0[:, 0:512], self.ps0[:, 512:1024],
                    self.ps1[:, 0:512], self.ps1[:, 512:1024]]
        self.trot = [self.pr[:].bitcast(BF16)[:, 0:128],
                     self.pg[:].bitcast(BF16)[:, 0:128],
                     self.pav[:].bitcast(BF16)[:, 0:128],
                     self.psm[:].bitcast(BF16)[:, 0:128]]

    def _sems(self):
        E = self._ctx.enter_context
        self.sem = {}
        for n in ["syn", "gpd", "pe", "act", "dv", "pet"]:
            self.sem[n] = E(self.nc.semaphore(n))
            self.cnt[n] = 0
        self.ccs = {}
        for l in self.layers:
            for tag in ["xn", "at"]:
                self.ccs[(l, tag)] = E(self.nc.semaphore(f"cc_{tag}{l}"))

    def inc(self, inst, s, n=1):
        inst.then_inc(self.sem[s], n)
        self.cnt[s] += n
        return self.cnt[s]

    def w(self, eng, s, v=None):
        eng.wait_ge(self.sem[s], self.cnt[s] if v is None else v)

    def _mm_groups(self, n, emit_mm, evac, pre_wait=None, evac_eng_of=None,
                   rot=None, evac_pre_wait=None, extra_release=None):
        p = self
        rot = rot or p.rot
        eng_of = evac_eng_of or (lambda k: "dv")
        pe0 = p.cnt["pe"]
        pe_val = [pe0 + k + 1 for k in range(n)]
        ev_val = [None] * n
        run = {"dv": p.cnt["dv"], "act": p.cnt["act"]}
        order = {"dv": [], "act": []}
        for k in range(n):
            e = eng_of(k)
            order[e].append(k)
        for e in ["dv", "act"]:
            for j, k in enumerate(order[e]):
                ev_val[k] = (e, run[e] + j + 1)

        @p.block.tensor
        def _(t):
            if pre_wait is not None:
                pre_wait(t)
            else:
                p.w(t, "dv")
                p.w(t, "act")
            for k in range(n):
                if k >= 4:
                    e, v = ev_val[k - 4]
                    p.w(t, e, v)
                    if extra_release is not None:
                        e2, v2 = extra_release(k - 4)
                        p.w(t, e2, v2)
                inst = emit_mm(t, k, rot[k % 4])
                inst.then_inc(p.sem["pe"], 1)
        p.cnt["pe"] = pe_val[-1]

        if order["dv"]:
            @p.block.vector
            def _(v):
                if evac_pre_wait is not None:
                    evac_pre_wait(v)
                for k in order["dv"]:
                    p.w(v, "pe", pe_val[k])
                    evac(v, k, rot[k % 4]).then_inc(p.sem["dv"], 1)
            p.cnt["dv"] += len(order["dv"])
        if order["act"]:
            @p.block.scalar
            def _(a):
                if evac_pre_wait is not None:
                    evac_pre_wait(a)
                for k in order["act"]:
                    p.w(a, "pe", pe_val[k])
                    evac(a, k, rot[k % 4]).then_inc(p.sem["act"], 1)
            p.cnt["act"] += len(order["act"])
        return pe_val, ev_val

    def _transposes(self, pairs, pre_wait=None):
        p = self
        n = len(pairs)
        pe0 = p.cnt["pe"]
        pe_val = [pe0 + i + 1 for i in range(n)]
        dv0, act0 = p.cnt["dv"], p.cnt["act"]
        ev_val = [("dv", dv0 + i // 2 + 1) if i % 2 == 0 else
                  ("act", act0 + i // 2 + 1) for i in range(n)]

        @p.block.tensor
        def _(t):
            if pre_wait is not None:
                pre_wait(t)
            else:
                p.w(t, "dv")
                p.w(t, "act")
            for i, (src, dst) in enumerate(pairs):
                if i >= 4:
                    e, v = ev_val[i - 4]
                    p.w(t, e, v)
                sl = p.trot[i % 4]
                rows, cols = src.shape[0], src.shape[1]
                inst = t.transpose(sl[0:cols, 0:rows], src,
                                   p.ident[0:rows, 0:rows])
                inst.then_inc(p.sem["pe"], 1)
        p.cnt["pe"] = pe_val[-1]

        @p.block.vector
        def _(v):
            for i in range(0, n, 2):
                src, dst = pairs[i]
                p.w(v, "pe", pe_val[i])
                v.tensor_copy(dst, p.trot[i % 4][0:dst.shape[0], 0:dst.shape[1]]
                              ).then_inc(p.sem["dv"], 1)
        p.cnt["dv"] = dv0 + (n + 1) // 2

        @p.block.scalar
        def _(a):
            for i in range(1, n, 2):
                src, dst = pairs[i]
                p.w(a, "pe", pe_val[i])
                a.activation(dst, p.trot[i % 4][0:dst.shape[0], 0:dst.shape[1]],
                             AF.Copy).then_inc(p.sem["act"], 1)
        p.cnt["act"] = act0 + n // 2
        return pe_val, ev_val

    def _go(self, fn, *a):
        self._phn += 1
        if self._phn <= self.limit:
            fn(*a)

    def _emit(self):
        self._prologue()
        for l in self.layers:
            self._go(self._ph_ln, l)
            self._go(self._ph_xnT_own)
            if l == 0:
                self._go(self._ph_gate)
                self._go(self._ph_moe)
            else:
                self._go(self._ph_ffn1)
            self._go(self._ph_stages, l)
            self._go(self._ph_stab, l)
            self._go(self._ph_attn, l)
            self._go(self._ph_attn_out, l)
        self._epilogue()

    def _prologue(self):
        p = self

        @p.block.sync
        def _(s):
            for n, dst in [("ident", p.ident), ("onesb", p.onesb),
                           ("onesc", p.onesc), ("onesf", p.onesf),
                           ("hsel", p.hsel), ("me", p.mes)]:
                p.inc(s.dma_start(out=dst[:], in_=p.din[n]), "syn", 16)
            p.inc(s.dma_start(
                out=p.x[:].rearrange("p (t d) -> p t d", t=NT),
                in_=p.din["x"].rearrange("(t p) d -> p t d", p=128)), "syn", 16)

    def _epilogue(self):
        p = self

        @p.block.sync
        def _(s):
            p.w(s, "dv")
            p.w(s, "act")
            p.w(s, "pe")
            p.inc(s.dma_start(
                out=p.d_y.rearrange("(t p) d -> p t d", p=128),
                in_=p.x[:].rearrange("p (t d) -> p t d", t=NT)), "syn", 16)
            p.w(s, "syn")

    def _ph_ln(self, l):
        p = self
        st = p.stats

        @p.block.vector
        def _(v):
            p.w(v, "syn")
            p.w(v, "act")
            p.w(v, "pe")
            p.w(v, "dv")
            for t in range(NT):
                inst = v.tensor_reduce(st[:, t:t + 1],
                                       p.x[:, 1024 * t:1024 * (t + 1)], AX, OP.add)
            p.inc(inst, "dv", 1)

        @p.block.scalar
        def _(a):
            p.w(a, "dv")
            p.w(a, "act")
            for t in range(NT):
                xt = p.x[:, 1024 * t:1024 * (t + 1)]
                inst = a.activation(p.qsq[:].bitcast(F32)[:, 0:512], xt[:, 0:512],
                                    AF.Square, accum_out=st[:, 32 + t:33 + t])
                p.inc(inst, "act", 1)
                p.w(a, "act")
                inst = a.activation(p.qsq[:].bitcast(F32)[:, 512:1024],
                                    xt[:, 512:1024], AF.Square,
                                    accum_out=st[:, 48 + t:49 + t])
                p.inc(inst, "act", 1)
                p.w(a, "act")

        @p.block.vector
        def _(v):
            p.w(v, "act")
            v.tensor_scalar(st[:, 16:32], st[:, 0:16], -1.0 / D, None, OP.mult)
            inst = v.tensor_add(st[:, 64:80], st[:, 32:48], st[:, 48:64])
            p.inc(inst, "dv", 1)
            p.w(v, "dv")
            v.tensor_scalar(st[:, 64:80], st[:, 64:80], 1.0 / D, None, OP.mult)
            inst = v.tensor_mul(st[:, 80:96], st[:, 16:32], st[:, 16:32])
            p.inc(inst, "dv", 1)
            p.w(v, "dv")
            inst = v.tensor_sub(st[:, 64:80], st[:, 64:80], st[:, 80:96])
            p.inc(inst, "dv", 1)

        @p.block.scalar
        def _(a):
            p.w(a, "dv")
            inst = a.activation(st[:, 80:96], st[:, 64:80], AF.Ln, bias=np.float32(1e-5))
            p.inc(inst, "act", 1)
            p.w(a, "act")
            inst = a.activation(st[:, 64:80], st[:, 80:96], AF.Exp, scale=np.float32(-0.5))
            p.inc(inst, "act", 1)

        @p.block.vector
        def _(v):
            p.w(v, "act")
            inst = v.tensor_mul(st[:, 16:32], st[:, 16:32], st[:, 64:80])
            p.inc(inst, "dv", 1)
            p.w(v, "dv")
            for t in range(NT):
                inst = v.tensor_scalar(p.pd[:, 1024 * t:1024 * (t + 1)],
                                       p.x[:, 1024 * t:1024 * (t + 1)],
                                       st[:, 16 + t:17 + t], st[:, 64 + t:65 + t],
                                       OP.add, OP.mult)
            p.inc(inst, "dv", 1)
        self.ln_done_dv = self.cnt["dv"]

        @p.block.sync
        def _(s):
            p.w(s, "dv")
            p.inc(s.dma_start(
                out=self.dsc[("xnp", l)].rearrange("(t p) d -> p t d", p=128),
                in_=p.pd[:, 0:NT * 1024].rearrange("p (t d) -> p t d", t=NT)),
                "syn", 16)
        self.xnp_out_syn = self.cnt["syn"]

        @p.block.gpsimd
        def _(g):
            p.w(g, "syn", self.xnp_out_syn)
            g.collective_compute(
                "AllGather", OP.bypass, replica_groups=GROUPS,
                ins=[self.dsc[("xnp", l)].opt()],
                outs=[self.dsc[("xng", l)].opt()],
            ).then_inc(self.ccs[(l, "xn")])

    def _ph_xnT_own(self):
        p = self
        pairs = []
        for tt in range(NT):
            for dc in range(ND):
                src = p.pd[:, 1024 * tt + 128 * dc:1024 * tt + 128 * (dc + 1)]
                dst = p.pt[:, 2048 * dc + 128 * tt:2048 * dc + 128 * (tt + 1)]
                pairs.append((src, dst))
        self._transposes(pairs)
        self.xnT_dv = self.cnt["dv"]
        self.xnT_act = self.cnt["act"]

    def _ph_gate(self):
        p = self

        @p.block.gpsimd
        def _(g):
            p.w(g, "gpd")
            p.w(g, "syn", self.xnp_out_syn)
            p.w(g, "pe")
            p.inc(g.dma_start(out=p.gw[:].rearrange("p (dc e) -> p dc e", dc=8),
                              in_=p.din["gatew"][0:D, :]
                              .rearrange("(dc p) e -> p dc e", p=128)), "gpd", 16)
            p.inc(g.dma_start(out=p.grow[:], in_=p.din["gatew"][D:D + 1, :]),
                  "gpd", 16)
            p.inc(g.dma_start(out=p.b2s[:], in_=p.din["b2"]), "gpd", 16)
            p.w(g, "gpd")
        gate_gpd = self.cnt["gpd"]
        xnT_dv, xnT_act = self.xnT_dv, self.xnT_act

        def mk_mm(t, k, ps):
            it = k
            ps8 = ps[0:128, 0:8]
            for dc in range(ND):
                t.matmul(ps8,
                         p.pt[:, 2048 * dc + 128 * it:2048 * dc + 128 * (it + 1)],
                         p.gw[:, 8 * dc:8 * (dc + 1)], start=(dc == 0), stop=False)
            return t.matmul(ps8, p.onesb[0:1, 0:128], p.grow[0:1, :],
                            start=False, stop=True)

        def mk_ev(v, k, ps):
            return v.tensor_copy(p.gl[:, 8 * k:8 * (k + 1)], ps[0:128, 0:8])

        def pw(t):
            p.w(t, "gpd", gate_gpd)
            p.w(t, "dv", xnT_dv)
            p.w(t, "act", xnT_act)
        self._mm_groups(NT, mk_mm, mk_ev, pre_wait=pw)

        gst = p.gst

        def step_v(v, fn):
            inst = None
            for g in range(NT):
                inst = fn(v, g)
            p.inc(inst, "dv", 1)
            p.w(v, "dv")

        @p.block.vector
        def _(v):
            p.w(v, "dv")
            step_v(v, lambda v, g: v.reduce_max(gst[:, g:g + 1],
                                                p.gl[:, 8 * g:8 * g + 8], AX))
            step_v(v, lambda v, g: v.tensor_scalar(
                p.gl[:, 8 * g:8 * g + 8], p.gl[:, 8 * g:8 * g + 8],
                gst[:, g:g + 1], None, OP.subtract))
        gate_sub_dv = self.cnt["dv"]

        @p.block.scalar
        def _(a):
            p.w(a, "dv", gate_sub_dv)
            for g in range(NT):
                inst = a.activation(p.ge[:, 8 * g:8 * g + 8],
                                    p.gl[:, 8 * g:8 * g + 8], AF.Exp,
                                    accum_out=gst[:, 16 + g:17 + g])
            p.inc(inst, "act", 1)
        gate_exp_act = self.cnt["act"]

        @p.block.vector
        def _(v):
            p.w(v, "act", gate_exp_act)
            step_v(v, lambda v, g: v.reciprocal(gst[:, g:g + 1],
                                                gst[:, 16 + g:17 + g]))
            step_v(v, lambda v, g: v.tensor_scalar(
                p.ge[:, 8 * g:8 * g + 8], p.ge[:, 8 * g:8 * g + 8],
                gst[:, g:g + 1], None, OP.mult))
            step_v(v, lambda v, g: v.reduce_max(gst[:, 32 + g:33 + g],
                                                p.ge[:, 8 * g:8 * g + 8], AX))
            step_v(v, lambda v, g: v.tensor_scalar(
                p.gl[:, 8 * g:8 * g + 8], p.ge[:, 8 * g:8 * g + 8],
                gst[:, 32 + g:33 + g], None, OP.is_ge))
            step_v(v, lambda v, g: v.scalar_tensor_tensor(
                p.gm[:, 8 * g:8 * g + 8], p.gl[:, 8 * g:8 * g + 8], -1e30,
                p.ge[:, 8 * g:8 * g + 8], OP.mult, OP.add))
            step_v(v, lambda v, g: v.reduce_max(gst[:, 48 + g:49 + g],
                                                p.gm[:, 8 * g:8 * g + 8], AX))
            step_v(v, lambda v, g: v.tensor_scalar(
                p.gl[:, 8 * g:8 * g + 8], p.ge[:, 8 * g:8 * g + 8],
                gst[:, 48 + g:49 + g], None, OP.is_ge))
            step_v(v, lambda v, g: v.tensor_mul(
                p.cw[:, 8 * g:8 * g + 8], p.ge[:, 8 * g:8 * g + 8],
                p.gl[:, 8 * g:8 * g + 8]))
            inst = v.memset(p.cw9[:], 1.0)
            p.inc(inst, "dv", 1)
            p.w(v, "dv")
            step_v(v, lambda v, g: v.tensor_copy(
                p.cw9[:, 16 * g:16 * g + 8], p.cw[:, 8 * g:8 * g + 8]))
        cw9_dv = self.cnt["dv"]

        pairs = [(p.cw9[:, 16 * it:16 * it + 9],
                  p.cwT[0:9, 128 * it:128 * (it + 1)]) for it in range(NT)]

        def pw2(t):
            p.w(t, "dv", cw9_dv)
            p.w(t, "act")
        self._transposes(pairs, pre_wait=pw2)
        cwT_dv, cwT_act = self.cnt["dv"], self.cnt["act"]

        def t2_mm(t, k, ps):
            it, es = k // 2, k % 2
            return t.matmul(ps, p.cwT[0:9, 128 * it:128 * (it + 1)],
                            p.b2s[0:9, 512 * es:512 * (es + 1)],
                            start=True, stop=True)

        def t2_ev(v, k, ps):
            it, es = k // 2, k % 2
            xs = p.x[:, 1024 * it + 512 * es:1024 * it + 512 * (es + 1)]
            return v.tensor_tensor(xs, ps, xs, OP.add)

        def pw3(t):
            p.w(t, "gpd", gate_gpd)
            p.w(t, "dv", cwT_dv)
            p.w(t, "act", cwT_act)
        self._mm_groups(NT * 2, t2_mm, t2_ev, pre_wait=pw3)

    def _ph_moe(self):
        marks = {}
        for ex in range(9):
            self._moe_expert(ex, marks)

    def _moe_expert(self, ex, marks):
        p = self
        slot = ex % 2

        @p.block.gpsimd
        def _(g):
            if ex > 0:
                p.w(g, "pe", marks[("h1", ex - 1)])
            else:
                p.w(g, "syn", self.xnp_out_syn)
            p.inc(g.dma_start(
                out=p.pb[:].rearrange("p (dc f) -> p dc f", dc=8),
                in_=p.din["w1"][ex].rearrange("(dc p) f -> p dc f", p=128)),
                "gpd", 16)
            marks[("w1dma", ex)] = p.cnt["gpd"]
            p.w(g, "gpd")
            if ex > 1:
                p.w(g, "pe", marks[("w2", ex - 2)])
            p.inc(g.dma_start(
                out=p.pd[:, 8192 * slot:8192 * (slot + 1)]
                .rearrange("p (dc e) -> p dc e", dc=8),
                in_=p.din["w2"][ex].rearrange("(dc p) e -> p dc e", p=128)),
                "gpd", 16)
            marks[("w2dma", ex)] = p.cnt["gpd"]
            p.w(g, "gpd")
            if ex > 1:
                p.w(g, "dv", marks[("h1evd", ex - 2)])
                p.w(g, "act", marks[("h1eva", ex - 2)])
            p.inc(g.dma_start(
                out=p.b1s[:, 16 * slot:16 * (slot + 1)],
                in_=p.din["b1"][ex]), "gpd", 16)
            marks[("b1dma", ex)] = p.cnt["gpd"]
            p.w(g, "gpd")

        for half in range(2):
            self._moe_half(ex, half, marks)
        marks[("h1", ex)] = marks[("h1_last",)]
        marks[("w2", ex)] = marks[("w2_last",)]
        marks[("h1evd", ex)] = marks[("h1evd_last",)]
        marks[("h1eva", ex)] = marks[("h1eva_last",)]

    def _moe_half(self, ex, half, marks):
        p = self
        slot = ex % 2
        xnT_dv, xnT_act = self.xnT_dv, self.xnT_act
        w1dma = marks[("w1dma", ex)]
        w2dma = marks[("w2dma", ex)]
        b1dma = marks[("b1dma", ex)]

        def h1_mm(t, k, ps):
            fc, isg = k // 2, k % 2
            i0 = 1024 * half + 512 * isg
            inst = None
            for dc in range(ND):
                inst = t.matmul(
                    ps, p.pb[:, 2048 * dc + 128 * fc:2048 * dc + 128 * (fc + 1)],
                    p.pt[:, 2048 * dc + i0:2048 * dc + i0 + 512],
                    start=(dc == 0), stop=(dc == ND - 1))
            return inst

        def h1_ev(eng, k, ps):
            fc, isg = k // 2, k % 2
            if fc < 8:
                dst = p.pc[:, 1024 * fc + 512 * isg:1024 * fc + 512 * (isg + 1)]
                return eng.activation(dst, ps, AF.Silu,
                                      bias=p.b1s[:, 16 * slot + fc:16 * slot + fc + 1])
            dst = p.pe_[:, 1024 * (fc - 8) + 512 * isg:
                        1024 * (fc - 8) + 512 * (isg + 1)]
            return eng.tensor_scalar(dst, ps,
                                     p.b1s[:, 16 * slot + fc:16 * slot + fc + 1],
                                     None, OP.add)

        def h1_pw(t):
            p.w(t, "gpd", w1dma)
            p.w(t, "dv")
            p.w(t, "act")

        def h1_evw(eng):
            p.w(eng, "gpd", b1dma)

        self._mm_groups(32, h1_mm, h1_ev, pre_wait=h1_pw, evac_pre_wait=h1_evw,
                        evac_eng_of=lambda k: "act" if (k // 2) < 8 else "dv")
        marks[("h1_last",)] = self.cnt["pe"]
        marks[("h1evd_last",)] = self.cnt["dv"]
        marks[("h1eva_last",)] = self.cnt["act"]

        @p.block.vector
        def _(v):
            p.w(v, "dv")
            p.w(v, "act")
            for fc in range(8):
                inst = v.tensor_mul(p.pc[:, 1024 * fc:1024 * (fc + 1)],
                                    p.pc[:, 1024 * fc:1024 * (fc + 1)],
                                    p.pe_[:, 1024 * fc:1024 * (fc + 1)])
            p.inc(inst, "dv", 1)
        g_dv = self.cnt["dv"]

        def w2_mm(t, k, ps):
            itl, es = k // 2, k % 2
            inst = None
            for fc in range(ND):
                inst = t.matmul(
                    ps, p.pc[:, 1024 * fc + 128 * itl:1024 * fc + 128 * (itl + 1)],
                    p.pd[:, 8192 * slot + 1024 * fc + 512 * es:
                         8192 * slot + 1024 * fc + 512 * (es + 1)],
                    start=(fc == 0), stop=(fc == ND - 1))
            return inst

        def w2_ev(v, k, ps):
            itl, es = k // 2, k % 2
            it = 8 * half + itl
            xs = p.x[:, 1024 * it + 512 * es:1024 * it + 512 * (es + 1)]
            if ex < 8:
                return v.scalar_tensor_tensor(
                    xs, ps, p.cw[:, 8 * it + ex:8 * it + ex + 1], xs,
                    OP.mult, OP.add)
            return v.tensor_tensor(xs, ps, xs, OP.add)

        def w2_pw(t):
            p.w(t, "gpd", w2dma)
            p.w(t, "dv", g_dv)
            p.w(t, "act")
        self._mm_groups(16, w2_mm, w2_ev, pre_wait=w2_pw)
        marks[("w2_last",)] = self.cnt["pe"]

    def _ph_ffn1(self):
        p = self
        wm = {"prev_s0": None, "prev_s1": None}

        @p.block.gpsimd
        def _(g):
            p.w(g, "syn", self.xnp_out_syn)
            p.inc(g.dma_start(out=p.ffbs[:], in_=p.din["ffb"]), "gpd", 16)
            p.w(g, "gpd")
        bias_gpd = self.cnt["gpd"]

        for half in range(2):
            @p.block.gpsimd
            def _(g, half=half):
                if wm["prev_s0"] is not None:
                    p.w(g, "pe", wm["prev_s0"])
                p.inc(g.dma_start(
                    out=p.pd[:, 0:8192].rearrange("p (dc e) -> p dc e", dc=8),
                    in_=p.din["ffw"].rearrange("(dc p) e -> p dc e", p=128)),
                    "gpd", 16)
                wm[("dma", 0)] = p.cnt["gpd"]
                if half == 0:
                    p.inc(g.dma_start(
                        out=p.pd[:, 8192:16384].rearrange("p (dc e) -> p dc e", dc=8),
                        in_=p.din["ffa1"].rearrange("(dc p) e -> p dc e", p=128)),
                        "gpd", 16)
                    wm[("dma", 1)] = p.cnt["gpd"]
                p.w(g, "gpd")

            for wi in range(3):
                sl = 0 if wi != 1 else 1
                src = [p.pt, p.pc, p.pe_][wi]
                if wi == 2:
                    @p.block.gpsimd
                    def _(g):
                        p.w(g, "pe", wm[("used0",)])
                        p.inc(g.dma_start(
                            out=p.pd[:, 0:8192].rearrange("p (dc e) -> p dc e", dc=8),
                            in_=p.din["ffa2"].rearrange("(dc p) e -> p dc e", p=128)),
                            "gpd", 16)
                        wm[("dma", 0)] = p.cnt["gpd"]
                        p.w(g, "gpd")
                wdma = wm[("dma", sl)]

                def mk_mm(t, k, ps, wi=wi, sl=sl, src=src, half=half):
                    oc, isg = k // 2, k % 2
                    inst = None
                    for dc in range(ND):
                        if wi == 0:
                            rhs = src[:, 2048 * dc + 1024 * half + 512 * isg:
                                      2048 * dc + 1024 * half + 512 * (isg + 1)]
                        else:
                            rhs = src[:, 1024 * dc + 512 * isg:
                                      1024 * dc + 512 * (isg + 1)]
                        inst = t.matmul(
                            ps, p.pd[:, 8192 * sl + 1024 * dc + 128 * oc:
                                     8192 * sl + 1024 * dc + 128 * (oc + 1)],
                            rhs, start=(dc == 0), stop=(dc == ND - 1))
                    return inst

                if wi < 2:
                    dst = [p.pc, p.pe_][wi]

                    def mk_ev(v, k, ps, dst=dst):
                        oc, isg = k // 2, k % 2
                        return v.tensor_copy(
                            dst[:, 1024 * oc + 512 * isg:1024 * oc + 512 * (isg + 1)],
                            ps)
                    eng_of = lambda k: "dv"
                else:
                    def mk_ev(a, k, ps):
                        oc, isg = k // 2, k % 2
                        return a.activation(
                            p.pc[:, 1024 * oc + 512 * isg:1024 * oc + 512 * (isg + 1)],
                            ps, AF.Tanh, bias=p.ffbs[:, oc:oc + 1])
                    eng_of = lambda k: "act"

                def pw(t, wdma=wdma):
                    p.w(t, "gpd", max(wdma, bias_gpd))
                    p.w(t, "dv")
                    p.w(t, "act")
                self._mm_groups(16, mk_mm, mk_ev, pre_wait=pw, evac_eng_of=eng_of)
                if wi == 0:
                    wm[("used0",)] = self.cnt["pe"]
            wm["prev_s0"] = self.cnt["pe"]

            pairs = []
            dsts = []
            for itl in range(8):
                it = 8 * half + itl
                for oc in range(ND):
                    pairs.append((p.pc[:, 1024 * oc + 128 * itl:
                                       1024 * oc + 128 * (itl + 1)], None))
                    dsts.append(p.x[:, 1024 * it + 128 * oc:
                                    1024 * it + 128 * (oc + 1)])
            self._transpose_add(pairs, dsts)

    def _transpose_add(self, pairs, dsts):
        p = self
        n = len(pairs)
        pe0 = p.cnt["pe"]
        pe_val = [pe0 + i + 1 for i in range(n)]
        dv0 = p.cnt["dv"]
        ev_val = [dv0 + i + 1 for i in range(n)]

        @p.block.tensor
        def _(t):
            p.w(t, "dv")
            p.w(t, "act")
            for i, (src, _d) in enumerate(pairs):
                if i >= 4:
                    p.w(t, "dv", ev_val[i - 4])
                inst = t.transpose(p.trot[i % 4], src, p.ident[:])
                inst.then_inc(p.sem["pe"], 1)
        p.cnt["pe"] = pe_val[-1]

        @p.block.vector
        def _(v):
            for i in range(n):
                p.w(v, "pe", pe_val[i])
                xs = dsts[i]
                v.tensor_tensor(xs, p.trot[i % 4], xs, OP.add).then_inc(
                    p.sem["dv"], 1)
        p.cnt["dv"] = dv0 + n

    def _ph_stages(self, l):
        p = self
        L0 = (l == 0)

        @p.block.gpsimd
        def _(g):
            p.w(g, "pe")
            for wi, wn in enumerate(["wq", "wk", "wv"]):
                p.inc(g.dma_start(
                    out=p.wqkv[:, 2048 * wi:2048 * (wi + 1)]
                    .rearrange("p (dc e) -> p dc e", dc=8),
                    in_=p.din[wn + ("0" if L0 else "1")]
                    .rearrange("(dc p) e -> p dc e", p=128)), "gpd", 16)
            if L0:
                p.inc(g.dma_start(out=p.bqs[:], in_=p.din["bq0"]), "gpd", 16)
                p.inc(g.dma_start(out=p.bks[:], in_=p.din["bk0"]), "gpd", 16)
                p.inc(g.dma_start(out=p.bvrow[:], in_=p.din["bv0"]), "gpd", 16)
            p.w(g, "gpd")
        wdma = self.cnt["gpd"]

        @p.block.vector
        def _(v):
            p.w(v, "dv")
            v.memset(p.qn[:, 0:1], 0.0)
            inst = v.memset(p.kn[:, 0:1], 0.0)
            p.inc(inst, "dv", 1)

        for s in range(self.stage_count):
            self._stage(l, s, wdma)
        self._qres_done = self.cnt["syn"]

    def _stage(self, l, s, wdma):
        p = self
        L0 = (l == 0)
        xng = self.dsc[("xng", l)]

        pairs = []
        for tt in range(NT):
            for dc in range(ND):
                src = p.roll[tt % 2][:, 128 * dc:128 * (dc + 1)]
                dst = p.pt[:, 2048 * dc + 128 * tt:2048 * dc + 128 * (tt + 1)]
                pairs.append((src, dst))
        n = len(pairs)
        pe0 = p.cnt["pe"]
        pe_val = [pe0 + i + 1 for i in range(n)]
        dv0, act0 = p.cnt["dv"], p.cnt["act"]
        ev_val = [("dv", dv0 + i // 2 + 1) if i % 2 == 0 else
                  ("act", act0 + i // 2 + 1) for i in range(n)]
        syn0 = p.cnt["syn"]
        roll_syn = [syn0 + 16 * (tt + 1) for tt in range(NT)]

        pet0 = p.cnt["pet"]

        @p.block.sync
        def _(sy):
            sy.wait_ge(self.ccs[(l, "xn")], 1)
            for tt in range(NT):
                if tt >= 2:
                    p.w(sy, "pet", pet0 + tt - 1)
                inst = sy.dma_start(
                    out=p.roll[tt % 2],
                    in_=xng[2048 * s + 128 * tt:2048 * s + 128 * (tt + 1), :])
                p.inc(inst, "syn", 16)
        assert p.cnt["syn"] == roll_syn[-1]

        @p.block.tensor
        def _(t):
            p.w(t, "dv")
            p.w(t, "act")
            for i, (src, dst) in enumerate(pairs):
                tt = i // ND
                if i % ND == 0:
                    p.w(t, "syn", roll_syn[tt])
                if i >= 4:
                    e, v = ev_val[i - 4]
                    p.w(t, e, v)
                inst = t.transpose(p.trot[i % 4], src, p.ident[:])
                inst.then_inc(p.sem["pe"], 1)
                if i % ND == ND - 1:
                    t.nop().then_inc(p.sem["pet"], 1)
                    p.cnt["pet"] += 1
        p.cnt["pe"] = pe_val[-1]

        @p.block.vector
        def _(v):
            for i in range(0, n, 2):
                p.w(v, "pe", pe_val[i])
                v.tensor_copy(pairs[i][1], p.trot[i % 4]).then_inc(p.sem["dv"], 1)
        p.cnt["dv"] = dv0 + (n + 1) // 2

        @p.block.scalar
        def _(a):
            for i in range(1, n, 2):
                p.w(a, "pe", pe_val[i])
                a.activation(pairs[i][1], p.trot[i % 4], AF.Copy).then_inc(
                    p.sem["act"], 1)
        p.cnt["act"] = act0 + n // 2
        stage_dv, stage_act = p.cnt["dv"], p.cnt["act"]

        if self.stage_sub < 2:
            return
        for which in range(min(2, self.stage_sub - 1)):
            wofs = 2048 * which
            bias = (p.bqs if which == 0 else p.bks) if L0 else None
            nrun = p.qn if which == 0 else p.kn

            def qk_mm(t, k, ps, wofs=wofs):
                isg, cc = k // 2, k % 2
                inst = None
                for dc in range(ND):
                    inst = t.matmul(
                        ps, p.wqkv[:, wofs + 256 * dc + 128 * cc:
                                   wofs + 256 * dc + 128 * (cc + 1)],
                        p.pt[:, 2048 * dc + 512 * isg:2048 * dc + 512 * (isg + 1)],
                        start=(dc == 0), stop=(dc == ND - 1))
                return inst

            qsyn0 = p.cnt["syn"]

            def qk_ev(v, k, ps, which=which, bias=bias, s=s, qsyn0=qsyn0):
                isg, cc = k // 2, k % 2
                if which == 0:
                    if k >= 2:
                        p.w(v, "syn", qsyn0 + 16 * (k - 1))
                    dst = p.qbuf[:, 512 * (k % 2):512 * (k % 2 + 1)]
                else:
                    buf = p.pc if cc == 0 else p.pe_
                    dst = buf[:, 2048 * s + 512 * isg:2048 * s + 512 * (isg + 1)]
                if bias is not None:
                    return v.tensor_scalar(dst, ps, bias[:, cc:cc + 1],
                                           None, OP.add)
                return v.tensor_copy(dst, ps)

            def qk_evw(eng):
                p.w(eng, "gpd", wdma)

            do_sq = bool(self.qk_parts & 4)
            act0b = p.cnt["act"]
            sq_act = [act0b + k + 1 for k in range(8)]

            def qk_pw(t):
                p.w(t, "gpd", wdma)
                p.w(t, "dv", p.cnt["dv"])
                p.w(t, "act", p.cnt["act"])
            mm_pe, _ev = self._mm_groups(
                8, qk_mm, qk_ev, pre_wait=qk_pw, evac_pre_wait=qk_evw,
                extra_release=(lambda k: ("act", sq_act[k])) if do_sq else None)
            if which == 0 and (self.qk_parts & 2):
                qres = self.dsc[("qres", l)]

                @p.block.sync
                def _(sy, _ev=_ev, s=s, qres=qres, qsyn0=qsyn0):
                    for k in range(8):
                        isg, cc = k // 2, k % 2
                        p.w(sy, "dv", _ev[k][1])
                        inst = sy.dma_start(
                            out=qres[128 * cc:128 * (cc + 1),
                                     2048 * s + 512 * isg:2048 * s + 512 * (isg + 1)],
                            in_=p.qbuf[:, 512 * (k % 2):512 * (k % 2 + 1)])
                        p.inc(inst, "syn", 16)
                assert p.cnt["syn"] == qsyn0 + 16 * 8

            if not do_sq:
                continue
            pe_hs0 = p.cnt["pe"]
            hs_pe = [pe_hs0 + k + 1 for k in range(8)]
            dv0b = p.cnt["dv"]

            @p.block.scalar
            def _(a, mm_pe=mm_pe, bias=bias, hs_pe=hs_pe, _ev=_ev):
                for k in range(8):
                    cc = k % 2
                    p.w(a, "pe", mm_pe[k])
                    p.w(a, _ev[k][0], _ev[k][1])
                    if k >= 4:
                        p.w(a, "pe", hs_pe[k - 4])
                    dst = p.qsq[:, 512 * (k % 4):512 * (k % 4 + 1)]
                    if bias is not None:
                        inst = a.activation(dst, p.rot[k % 4], AF.Square,
                                            bias=bias[:, cc:cc + 1])
                    else:
                        inst = a.activation(dst, p.rot[k % 4], AF.Square)
                    inst.then_inc(p.sem["act"], 1)
            p.cnt["act"] = sq_act[-1]

            @p.block.tensor
            def _(t, sq_act=sq_act):
                p.w(t, "dv", dv0b)
                for k in range(8):
                    cc = k % 2
                    p.w(t, "act", sq_act[k])
                    if k >= 2 and cc == 0:
                        p.w(t, "dv", dv0b + 2 * (k // 2))
                    inst = t.matmul(p.psm[0:4, :], p.hsel[:, 4 * cc:4 * (cc + 1)],
                                    p.qsq[:, 512 * (k % 4):512 * (k % 4 + 1)],
                                    start=(cc == 0), stop=(cc == 1))
                    inst.then_inc(p.sem["pe"], 1)
            p.cnt["pe"] = hs_pe[-1]

            @p.block.vector
            def _(v, nrun=nrun, hs_pe=hs_pe):
                for pair in range(4):
                    p.w(v, "pe", hs_pe[2 * pair + 1])
                    inst = v.reduce_max(nrun[0:4, 1:2], p.psm[0:4, :], AX)
                    p.inc(inst, "dv", 1)
                    p.w(v, "dv")
                    inst = v.tensor_max(nrun[0:4, 0:1], nrun[0:4, 0:1],
                                        nrun[0:4, 1:2])
                    p.inc(inst, "dv", 1)
                    p.w(v, "dv")
            assert p.cnt["dv"] == dv0b + 8

        if self.stage_sub < 4:
            return
        def v_mm(t, k, ps):
            tt = k
            ps256 = ps[0:128, 0:256]
            inst = None
            for dc in range(ND):
                inst = t.matmul(
                    ps256, p.pt[:, 2048 * dc + 128 * tt:2048 * dc + 128 * (tt + 1)],
                    p.wqkv[:, 4096 + 256 * dc:4096 + 256 * (dc + 1)],
                    start=(dc == 0), stop=(dc == ND - 1 and not L0))
            if L0:
                inst = t.matmul(ps256, p.onesb[0:1, 0:128], p.bvrow[0:1, :],
                                start=False, stop=True)
            return inst

        def v_ev(a, k, ps):
            jt = 16 * s + k
            return a.activation(p.pd[:, 256 * jt:256 * (jt + 1)],
                                ps[0:128, 0:256], AF.Copy)

        def v_pw(t):
            p.w(t, "gpd", wdma)
            p.w(t, "dv", p.cnt["dv"])
            p.w(t, "act", p.cnt["act"])
        self._mm_groups(NT, v_mm, v_ev, pre_wait=v_pw,
                        evac_eng_of=lambda k: "act")

    def _ph_stab(self, l):
        p = self
        nb = self.dsc[("nb", l)]

        if l == 0:
            @p.block.sync
            def _(sy):
                p.w(sy, "pe")
                p.w(sy, "act")
                p.inc(sy.dma_start(
                    out=p.msk.rearrange("p (m i) -> p m i", m=4),
                    in_=p.din["masks"].rearrange("m p i -> p m i")), "syn", 16)

        @p.block.sync
        def _(sy):
            p.w(sy, "dv")
            p.inc(sy.dma_start(out=nb[0:4, :], in_=p.qn[0:4, 0:1]), "syn", 16)
            p.inc(sy.dma_start(out=nb[4:8, :], in_=p.kn[0:4, 0:1]), "syn", 16)
            p.w(sy, "syn")
            for tq in range(2):
                p.inc(sy.dma_start(
                    out=p.nb2[0:2, 2 * tq:2 * tq + 2],
                    in_=nb[4 * tq:4 * tq + 4, :]
                    .rearrange("(pp two) one -> pp (two one)", pp=2),
                    allow_slow_non_contiguous=True), "syn", 16)

        @p.block.vector
        def _(v):
            p.w(v, "syn")
            v.reduce_max(p.nb2[0:2, 4:5], p.nb2[0:2, 0:2], AX)
            inst = v.reduce_max(p.nb2[0:2, 5:6], p.nb2[0:2, 2:4], AX)
            p.inc(inst, "dv", 1)
            p.w(v, "dv")
            inst = v.tensor_mul(p.nb2[0:2, 6:7], p.nb2[0:2, 4:5], p.nb2[0:2, 5:6])
            p.inc(inst, "dv", 1)

        @p.block.scalar
        def _(a):
            p.w(a, "dv")
            inst = a.activation(p.nb2[0:2, 7:8], p.nb2[0:2, 6:7], AF.Ln, bias=np.float32(1e-30))
            p.inc(inst, "act", 1)
            p.w(a, "act")
            inst = a.activation(p.nb2[0:2, 6:7], p.nb2[0:2, 7:8], AF.Exp,
                                scale=np.float32(0.5), bias=np.float32(-np.log(8.0)))
            p.inc(inst, "act", 1)

        @p.block.vector
        def _(v):
            p.w(v, "act")
            inst = v.tensor_scalar(p.ncp[0:2, 0:1], p.nb2[0:2, 6:7], -1.0,
                                   None, OP.mult)
            p.inc(inst, "dv", 1)

        tp = []
        @p.block.tensor
        def _(t):
            p.w(t, "dv")
            p.w(t, "act")
            inst = t.transpose(p.trot[0][0:1, 0:2], p.ncp[0:2, 0:1],
                               p.ident[0:2, 0:2])
            tp.append(p.inc(inst, "pe", 1))

        @p.block.vector
        def _(v):
            p.w(v, "pe", tp[0])
            inst = v.tensor_copy(p.ncp[0:1, 0:2], p.trot[0][0:1, 0:2])
            p.inc(inst, "dv", 1)

        bc = []
        @p.block.tensor
        def _(t):
            p.w(t, "dv")
            p.w(t, "act")
            inst = t.matmul(p.pg[0:128, 0:2], p.onesb[0:1, 0:128],
                            p.ncp[0:1, 0:2], start=True, stop=True)
            bc.append(p.inc(inst, "pe", 1))

        @p.block.vector
        def _(v):
            p.w(v, "pe", bc[0])
            inst = v.tensor_copy(p.cb[:, 0:2], p.pg[0:128, 0:2])
            p.inc(inst, "dv", 1)
        self.cb_dv = self.cnt["dv"]

    def _ph_attn(self, l):
        p = self
        L0 = (l == 0)
        prev_act = self.cnt["act"]
        prev_fin = self.cnt["dv"]
        self._pet0 = self.cnt["pet"]
        u = 0
        for pp in range(2):
            for ib in range(NIB):
                tiles = _l0_tiles(ib) if L0 else [(jt, None) for jt in range(NJT)]
                prev_act, prev_fin = self._attn_unit(l, u, pp, ib, tiles,
                                                     prev_act, prev_fin)
                u += 1
        self.attn_done = dict(act=prev_act, dv=prev_fin, pe=self.cnt["pe"])

    def _attn_unit(self, l, u, pp, ib, tiles, prev_act, prev_fin):
        p = self
        n = len(tiles)
        E = [p.e0, p.e1]
        psS = [p.ps0, p.ps1]
        kres = [p.pc, p.pe_][pp]
        cb_dv = self.cb_dv
        qres = self.dsc[("qres", l)]

        @p.block.sync
        def _(sy):
            if u >= 2:
                p.w(sy, "pet", self._pet0 + u - 1)
            else:
                p.w(sy, "pet")
                p.w(sy, "syn", self._qres_done)
            inst = sy.dma_start(
                out=p.qbuf[:, 512 * (u % 2):512 * (u % 2 + 1)],
                in_=qres[128 * pp:128 * (pp + 1), 512 * ib:512 * (ib + 1)])
            p.inc(inst, "syn", 16)
        q_syn = p.cnt["syn"]
        qb = p.qbuf[:, 512 * (u % 2):512 * (u % 2 + 1)]

        act0 = p.cnt["act"]
        act_ = [act0 + i + 1 for i in range(n)]
        dv0 = p.cnt["dv"]
        maskdv = {}
        dvk = dv0
        for i, (jt, m) in enumerate(tiles):
            if m is not None:
                dvk += 1
                maskdv[i] = dvk
        peS = [None] * n
        peAVS = [None] * n
        pe = p.cnt["pe"]
        for i in range(n):
            pe += 1
            peS[i] = pe
            if i >= 1:
                pe += 1
                peAVS[i - 1] = pe
        pe += 1
        peAVS[n - 1] = pe

        def emit_av(t, i):
            jt = tiles[i][0]
            first, last = (i == 0), (i == n - 1)
            t.wait_ge(p.sem["act"], act_[i])
            t.matmul(p.pav[0:64, :], p.pd[:, 256 * jt + 128 * pp:
                                          256 * jt + 128 * pp + 64],
                     E[i % 2][:, 0:512], start=first, stop=last,
                     tile_position=(0, 0))
            t.matmul(p.pav[64:128, :], p.pd[:, 256 * jt + 128 * pp + 64:
                                            256 * jt + 128 * pp + 128],
                     E[i % 2][:, 512:1024], start=first, stop=last,
                     tile_position=(0, 64))
            t.matmul(p.psm[0:1, :], p.onesc[:, 0:1], E[i % 2][:, 0:512],
                     start=first, stop=last, tile_position=(0, 0))
            inst = t.matmul(p.psm[64:65, :], p.onesc[:, 0:1],
                            E[i % 2][:, 512:1024], start=first, stop=last,
                            tile_position=(0, 64))
            inst.then_inc(p.sem["pe"], 1)

        @p.block.tensor
        def _(t):
            p.w(t, "dv", prev_fin)
            p.w(t, "act", prev_act)
            p.w(t, "dv", cb_dv)
            p.w(t, "syn", q_syn)
            for i, (jt, m) in enumerate(tiles):
                if i >= 2:
                    p.w(t, "act", act_[i - 2])
                t.matmul(psS[i % 2][:, 0:512], kres[0:64, 128 * jt:128 * (jt + 1)],
                         qb[0:64, :], start=True, stop=True, tile_position=(0, 0))
                inst = t.matmul(
                    psS[i % 2][:, 512:1024], kres[64:128, 128 * jt:128 * (jt + 1)],
                    qb[64:128, :], start=True, stop=True, tile_position=(64, 0))
                inst.then_inc(p.sem["pe"], 1)
                if i >= 1:
                    emit_av(t, i - 1)
            emit_av(t, n - 1)
            t.nop().then_inc(p.sem["pet"], 1)
            p.cnt["pet"] += 1
        p.cnt["pe"] = pe

        @p.block.vector
        def _(v):
            for i, (jt, m) in enumerate(tiles):
                if m is None:
                    continue
                p.w(v, "pe", peS[i])
                v.tensor_tensor(psS[i % 2][:, 0:512], psS[i % 2][:, 0:512],
                                p.msk[:, 512 * m:512 * (m + 1)], OP.add)
                inst = v.tensor_tensor(psS[i % 2][:, 512:1024],
                                       psS[i % 2][:, 512:1024],
                                       p.msk[:, 512 * m:512 * (m + 1)], OP.add)
                p.inc(inst, "dv", 1)
        assert p.cnt["dv"] == dvk

        @p.block.scalar
        def _(a):
            p.w(a, "dv", cb_dv)
            for i, (jt, m) in enumerate(tiles):
                if m is not None:
                    p.w(a, "dv", maskdv[i])
                else:
                    p.w(a, "pe", peS[i])
                if i >= 2:
                    p.w(a, "pe", peAVS[i - 2])
                inst = a.activation(E[i % 2][:], psS[i % 2][:], AF.Exp,
                                    bias=p.cb[:, pp:pp + 1], scale=np.float32(0.125))
                inst.then_inc(p.sem["act"], 1)
        p.cnt["act"] = act_[-1]

        @p.block.vector
        def _(v):
            p.w(v, "pe", peAVS[n - 1])
            v.reciprocal(p.rbuf[0:1, 0:512], p.psm[0:1, :])
            inst = v.reciprocal(p.rbuf[64:65, 0:512], p.psm[64:65, :])
            p.inc(inst, "dv", 1)
        dvrec = p.cnt["dv"]

        rpe = []
        @p.block.tensor
        def _(t):
            p.w(t, "dv", dvrec)
            t.matmul(p.pr[0:64, :], p.onesf[0:1, 0:64], p.rbuf[0:1, 0:512],
                     start=True, stop=True, tile_position=(0, 0))
            inst = t.matmul(p.pr[64:128, :], p.onesf[64:65, 0:64],
                            p.rbuf[64:65, 0:512], start=True, stop=True,
                            tile_position=(64, 64))
            rpe.append(p.inc(inst, "pe", 1))

        @p.block.vector
        def _(v):
            p.w(v, "pe", rpe[0])
            inst = v.tensor_copy(p.rrep[:], p.pr[:])
            p.inc(inst, "dv", 1)
            p.w(v, "dv")
            inst = v.tensor_tensor(
                p.pt[:, 8192 * pp + 512 * ib:8192 * pp + 512 * (ib + 1)],
                p.pav[:], p.rrep[:], OP.mult)
            p.inc(inst, "dv", 1)
        return p.cnt["act"], p.cnt["dv"]

    def _ph_attn_out(self, l):
        p = self
        L0 = (l == 0)
        atp, atg = self.dsc[("atp", l)], self.dsc[("atg", l)]

        @p.block.sync
        def _(sy):
            p.w(sy, "dv", self.attn_done["dv"])
            for pp in range(2):
                p.inc(sy.dma_start(out=atp[128 * pp:128 * (pp + 1), :],
                                   in_=p.pt[:, 8192 * pp:8192 * (pp + 1)]),
                      "syn", 16)
        atp_syn = self.cnt["syn"]

        @p.block.gpsimd
        def _(g):
            p.w(g, "syn", atp_syn)
            g.collective_compute(
                "AllGather", OP.bypass, replica_groups=GROUPS,
                ins=[atp.opt()], outs=[atg.opt()],
            ).then_inc(self.ccs[(l, "at")])
            g.wait_ge(self.ccs[(l, "at")], 1)
            p.w(g, "pe", self.attn_done["pe"])
            mev = g.value_load(p.mes[0:1, 0:1])
            gv = atg.rearrange("(dc pq) (r i) -> dc pq r i", pq=128, r=4)
            for dc in range(ND):
                inst = g.dma_start(out=p.pb[:, 2048 * dc:2048 * (dc + 1)],
                                   in_=gv[dc, :, mev, :])
                p.inc(inst, "gpd", 16)
            p.w(g, "gpd")
        ao_gpd = self.cnt["gpd"]

        if L0:
            @p.block.gpsimd
            def _(g):
                p.w(g, "pe", self.attn_done["pe"])
                p.inc(g.dma_start(
                    out=p.pc[:].rearrange("p (dc e) -> p dc e", dc=8),
                    in_=p.din["wo0"][0:D, :].rearrange("(dc p) e -> p dc e", p=128)),
                    "gpd", 16)
                p.inc(g.dma_start(out=p.worow[:], in_=p.din["wo0"][D:D + 1, :]),
                      "gpd", 16)
                p.w(g, "gpd")
            wo_gpd = self.cnt["gpd"]

            def o_mm(t, k, ps):
                it, es = k // 2, k % 2
                for dc in range(ND):
                    t.matmul(ps,
                             p.pb[:, 2048 * dc + 128 * it:2048 * dc + 128 * (it + 1)],
                             p.pc[:, 1024 * dc + 512 * es:1024 * dc + 512 * (es + 1)],
                             start=(dc == 0), stop=False)
                return t.matmul(ps, p.onesb[0:1, 0:128],
                                p.worow[0:1, 512 * es:512 * (es + 1)],
                                start=False, stop=True)

            def o_ev(v, k, ps):
                it, es = k // 2, k % 2
                xs = p.x[:, 1024 * it + 512 * es:1024 * it + 512 * (es + 1)]
                return v.tensor_tensor(xs, ps, xs, OP.add)

            def o_pw(t):
                p.w(t, "gpd", wo_gpd)
                p.w(t, "dv")
                p.w(t, "act")
            self._mm_groups(NT * 2, o_mm, o_ev, pre_wait=o_pw)
        else:
            pairs = []
            dsts = []
            for it in range(NT):
                for dc in range(ND):
                    pairs.append((p.pb[:, 2048 * dc + 128 * it:
                                       2048 * dc + 128 * (it + 1)], None))
                    dsts.append(p.x[:, 1024 * it + 128 * dc:
                                    1024 * it + 128 * (dc + 1)])

            @p.block.tensor
            def _(t):
                p.w(t, "gpd", ao_gpd)
            self._transpose_add(pairs, dsts)


_PROG_CACHE = {}


def _get_prog(layers):
    key = tuple(layers)
    if key not in _PROG_CACHE:
        _PROG_CACHE[key] = Prog(layers)
    return _PROG_CACHE[key]


def run_layers(layers, per_core):
    prog = _get_prog(layers)
    res = run_bass_kernel_spmd(prog.nc, per_core, core_ids=list(range(8)))
    return [np.asarray(r["y"], dtype=np.float32) for r in res.results]


def assemble(ys):
    out = np.empty((B, S, D), dtype=np.float32)
    for c in range(8):
        b, me = c // 4, c % 4
        out[b, 2048 * me:2048 * (me + 1)] = ys[c]
    return _unperm(out)


def kernel(**inputs):
    per_core = host_prep(inputs)
    ys = run_layers([0], per_core)
    for c in range(8):
        per_core[c]["x"] = ys[c]
    ys = run_layers([1], per_core)
    return assemble(ys)
